# version 18
# baseline (speedup 1.0000x reference)
import numpy as np
from contextlib import ExitStack
import concourse.bass as bass
import concourse.mybir as mybir
from concourse.bass_utils import run_bass_kernel_spmd

F32 = mybir.dt.float32
BF16 = mybir.dt.bfloat16
AF = mybir.ActivationFunctionType
ALU = mybir.AluOpType
AX = mybir.AxisListType

D = 2048
S = 2048
NCH = 16
NSUB = 4
TG = 1024
NG = 2
NE = 8
NL = 2
R = 64

MAGIC = 12582912.0
INV2PI = float(1.0 / (2.0 * np.pi))
TWOPI = float(2.0 * np.pi)
PIH = float(np.pi / 2.0)
GC = float(1.0 / np.e)
GL = float(0.5 - np.log(4.0 / 3.0))
GU = 0.5
EPS = 1e-5
f32i = np.float32


class Ct:
    def __init__(self, nc, name):
        self.h = nc.alloc_semaphore(name)
        self.n = 0

    def inc(self, inst, k=1):
        inst.then_inc(self.h, k)
        self.n += k
        return self.n


class Qt:

    def __init__(self, nc, name):
        self.hh = [nc.alloc_semaphore(name + "a"),
                   nc.alloc_semaphore(name + "b")]
        self.nn = [0, 0]
        self.cur = 0

    def chain(self, eng):
        eng.wait_ge(self.hh[self.cur], self.nn[self.cur])

    def inc(self, inst, k=16):
        inst.then_inc(self.hh[self.cur], k)
        self.nn[self.cur] += k
        self.cur ^= 1

    @property
    def snap(self):
        return ("s", self.nn[0], self.nn[1])

    @property
    def mark(self):
        i = self.cur ^ 1
        return ("m", i, self.nn[i])


def qwait(eng, ct, val):
    if isinstance(val, tuple):
        if val[0] == "s":
            if val[1]:
                eng.wait_ge(ct.hh[0], val[1])
            if val[2]:
                eng.wait_ge(ct.hh[1], val[2])
        else:
            eng.wait_ge(ct.hh[val[1]], val[2])
    elif val:
        eng.wait_ge(ct.h, val)


DEBUG_IO = [False]


def build_nc():
    nc = bass.Bass("TRN2", target_bir_lowering=False)

    xf_d = nc.dram_tensor("xf", [NSUB, 128, NCH * 512], F32,
                          kind="ExternalInput").ap()
    wkv_d = nc.dram_tensor("wkv", [32, 128, NCH * 128], F32,
                           kind="ExternalInput").ap()
    bt_d = nc.dram_tensor("bt", [128, 32], F32, kind="ExternalInput").ap()
    rope_d = nc.dram_tensor("rope", [4, 128, 2 * S], F32,
                            kind="ExternalInput").ap()
    wgu_d = nc.dram_tensor("wgu", [NE, NCH, 128, NCH * 256], F32,
                           kind="ExternalInput").ap()
    wd_d = nc.dram_tensor("wd", [NE, 8, 128, NCH * 256], F32,
                          kind="ExternalInput").ap()
    gua_d = nc.dram_tensor("gua", [NL, 128, 2 * NCH * R], F32,
                           kind="ExternalInput").ap()
    gub_d = nc.dram_tensor("gub", [NL, R, 2 * D], F32,
                           kind="ExternalInput").ap()
    da_d = nc.dram_tensor("da", [NL, 128, NCH * R], F32,
                          kind="ExternalInput").ap()
    db_d = nc.dram_tensor("db", [NL, R, D], F32, kind="ExternalInput").ap()
    wr_d = nc.dram_tensor("wr", [128, NCH * 8], F32,
                          kind="ExternalInput").ap()
    temp_d = nc.dram_tensor("temp", [1, 1], F32, kind="ExternalInput").ap()
    out_d = nc.dram_tensor("out", [D, S], F32, kind="ExternalOutput").ap()

    _ik = "ExternalOutput" if DEBUG_IO[0] else "Internal"
    x1_dram = nc.dram_tensor("x1scr", [D, S], F32, kind=_ik).ap()
    z_dram = nc.dram_tensor("zscr", [D, S], BF16, kind=_ik).ap()
    if DEBUG_IO[0]:
        dbgA_d = nc.dram_tensor("dbgA", [128, 16], F32,
                                kind="ExternalOutput").ap()
        dbgM_d = nc.dram_tensor("dbgM", [2, S], F32,
                                kind="ExternalOutput").ap()
        dbgW_d = nc.dram_tensor("dbgW", [8, S], BF16,
                                kind="ExternalOutput").ap()
        dbgL_d = nc.dram_tensor("dbgL", [8, S], F32,
                                kind="ExternalOutput").ap()
        dbgR_d = nc.dram_tensor("dbgR", [128, 96], F32,
                                kind="ExternalOutput").ap()
        dbgY_d = nc.dram_tensor("dbgY", [D, TG], F32,
                                kind="ExternalOutput").ap()
        dbgMid_d = nc.dram_tensor("dbgMid", [D, TG], BF16,
                                  kind="ExternalOutput").ap()

    id8_c = nc.inline_tensor(np.eye(8, dtype=np.float32), "id8c")
    id128_c = nc.inline_tensor(np.eye(128, dtype=np.float32), "id128c")
    import ml_dtypes as _mld
    sel_np = np.zeros((8, 1024), np.float32)
    for e in range(8):
        sel_np[e, e * 128:(e + 1) * 128] = 1.0
    sel_c = nc.inline_tensor(sel_np.astype(_mld.bfloat16), "selc")

    qS = Qt(nc, "qS")
    qG = Qt(nc, "qG")
    qA = Qt(nc, "qA")
    PE = Ct(nc, "cPE")
    VE = Ct(nc, "cVE")
    AC = Ct(nc, "cAC")
    PO = Ct(nc, "cPO")

    _es0 = ExitStack()
    ones_col = _es0.enter_context(nc.sbuf_tensor("ones_col", [128, 1], BF16))
    ones_row = _es0.enter_context(nc.sbuf_tensor("ones_row", [1, 128], F32))
    id8 = _es0.enter_context(nc.sbuf_tensor("id8", [8, 8], F32))
    id128 = _es0.enter_context(nc.sbuf_tensor("id128", [128, 128], F32))
    invT = _es0.enter_context(nc.sbuf_tensor("invT", [1, 1], F32))
    c2pi = _es0.enter_context(nc.sbuf_tensor("c2pi", [128, 1], F32))
    cmag = _es0.enter_context(nc.sbuf_tensor("cmag", [128, 1], F32))
    invT_bc = _es0.enter_context(nc.sbuf_tensor("invT_bc", [128, 1], F32))
    tempc = _es0.enter_context(nc.sbuf_tensor("tempc", [1, 1], F32))
    btb = _es0.enter_context(nc.sbuf_tensor("btb", [128, 32], F32))
    wrb = _es0.enter_context(nc.sbuf_tensor("wrb", [128, NCH * 8], BF16))
    A_acc = _es0.enter_context(nc.sbuf_tensor("A_acc", [128, 16], F32))
    wT = _es0.enter_context(nc.sbuf_tensor("wT", [8, S], BF16))
    psA = _es0.enter_context(nc.psum_tensor("psA", [128, 2048], F32))
    psB = _es0.enter_context(nc.psum_tensor("psB", [128, 2048], F32))
    block = _es0.enter_context(nc.Block())
    T_, V_, A_, P_, S_ = (block.tensor, block.vector, block.scalar,
                          block.gpsimd, block.sync)
    PS = [psA, psB]


    def f(sy):
        for dst, src in ((id8, id8_c), (id128, id128_c)):
            qS.inc(sy.dma_start(out=dst[:, :], in_=src.ap()[:, :]), 16)
        qS.inc(sy.dma_start(out=btb[:, :], in_=bt_d[:, :]), 16)
        qS.inc(sy.dma_start(out=tempc[:, :], in_=temp_d[:, :]), 16)
    S_(f)
    ld0 = qS.snap

    def f(g):
        qG.chain(g)
        qG.inc(g.dma_start(out=wrb[:, :], in_=wr_d[:, :]), 16)
    P_(f)
    wrb_ld = qG.mark

    def f(v):
        v.memset(ones_col[:], 1.0)
        v.memset(ones_row[:], 1.0)
        v.memset(c2pi[:], INV2PI)
        v.memset(cmag[:], MAGIC)
        qwait(v, qS, ld0)
        VE.inc(v.reciprocal(invT[:, :], tempc[:, :]))
    V_(f)
    m_ve = VE.n

    def f(t, m_ve=m_ve):
        t.wait_ge(VE.h, m_ve)
        PE.inc(t.matmul(psA[0:128, 0:1], ones_row[:, :], invT[:, :],
                        start=True, stop=True))
    T_(f)
    m_pe = PE.n

    def f(v, m_pe=m_pe):
        v.wait_ge(PE.h, m_pe)
        VE.inc(v.tensor_copy(invT_bc[:, :], psA[0:128, 0:1]))
    V_(f)
    invT_mark = VE.n

    def ln_stats(conv_src, bO, vrow, xb16, xq16, mrow, rrow, waits,
                 war_convert=None, bank_wait=None, conv_eng="v"):
        xb16_v = xb16[:].rearrange("p (c w) -> p c w", c=NCH)
        xq16_v = xq16[:].rearrange("p (c w) -> p c w", c=NCH)

        CE, Csect = (PO, P_) if conv_eng == "p" else (VE, V_)

        def f(p):
            for ct, val in waits:
                qwait(p, ct, val)
            if war_convert is not None:
                qwait(p, war_convert[0], war_convert[1])
            CE.inc(p.tensor_copy(xb16_v, conv_src))
        Csect(f)
        m_po = CE.n

        def f(a):
            for ct, val in waits:
                qwait(a, ct, val)
            if war_convert is not None:
                qwait(a, war_convert[0], war_convert[1])
            AC.inc(a.activation(xq16_v, conv_src, AF.Square))
        A_(f)
        m_ac = AC.n

        def f(t, m_po=m_po, m_ac=m_ac):
            t.wait_ge(CE.h, m_po)
            if bank_wait is not None:
                t.wait_ge(bank_wait[0].h, bank_wait[1])
            for ch in range(NCH):
                t.matmul(bO[0:1, 0:512], ones_col[:, :],
                         xb16[:, ch * 512:(ch + 1) * 512],
                         start=(ch == 0), stop=(ch == NCH - 1))
            t.wait_ge(AC.h, m_ac)
            for ch in range(NCH):
                inst = t.matmul(bO[0:1, 512:1024], ones_col[:, :],
                                xq16[:, ch * 512:(ch + 1) * 512],
                                start=(ch == 0), stop=(ch == NCH - 1))
            PE.inc(inst)
        T_(f)
        conv_pe = PE.n

        def f(v, conv_pe=conv_pe):
            v.wait_ge(PE.h, conv_pe)
            VE.inc(v.tensor_scalar(mrow, bO[0:1, 0:512], 1.0 / D, None,
                                   ALU.mult))
            v.wait_ge(VE.h, VE.n)
            VE.inc(v.tensor_tensor(vrow, mrow, mrow, ALU.mult))
            v.wait_ge(VE.h, VE.n)
            VE.inc(v.scalar_tensor_tensor(vrow, bO[0:1, 512:1024],
                                          1.0 / D, vrow,
                                          ALU.mult, ALU.subtract))
        V_(f)
        m_fin = VE.n

        def f(a, m_fin=m_fin):
            a.wait_ge(VE.h, m_fin)
            AC.inc(a.activation(vrow, vrow, AF.Ln, bias=f32i(EPS)))
            a.wait_ge(AC.h, AC.n)
            AC.inc(a.activation(rrow, vrow, AF.Exp, scale=f32i(-0.5)))
        A_(f)
        m_r = AC.n

        def f(t, m_r=m_r):
            t.wait_ge(AC.h, m_r)
            t.matmul(bO[:, 0:512], ones_row[:, :], mrow,
                     start=True, stop=True)
            PE.inc(t.matmul(bO[:, 512:1024], ones_row[:, :], rrow,
                            start=True, stop=True))
        T_(f)
        return {"conv_pe": conv_pe, "bc_pe": PE.n,
                "mb": bO[:, 0:512], "rb": bO[:, 512:1024]}

    with (
        nc.sbuf_tensor("m1", [1, S], F32) as m1,
        nc.sbuf_tensor("rstd1", [1, S], F32) as rstd1,
    ):
        with nc.sbuf_tensor("ln1b", [128, NCH * S], BF16) as ln1b:
            with (
                nc.sbuf_tensor("xsubA", [128, 2 * NCH * 512], F32) as xsub,
                nc.sbuf_tensor("xb16A", [128, NCH * 512], BF16) as xb16,
                nc.sbuf_tensor("xq16A", [128, NCH * 512], BF16) as xq16,
                nc.sbuf_tensor("mbcA", [128, 2 * 512], F32) as mbcA,
                nc.sbuf_tensor("rbcA", [128, 2 * 512], F32) as rbcA,
                nc.sbuf_tensor("varA", [1, 1024], F32) as varA,
            ):
                applyA = [None] * NSUB
                statA = [None] * NSUB
                for s in range(NSUB):
                    par = s % 2
                    xs = xsub[:, par * NCH * 512:(par + 1) * NCH * 512]
                    bO = PS[par]

                    def f(sy, s=s, xs=xs):
                        if s >= 2:
                            vm, pm = applyA[s - 2]
                            sy.wait_ge(VE.h, vm)
                            sy.wait_ge(PO.h, pm)
                        qS.chain(sy)
                        qS.inc(sy.dma_start(out=xs[:, :], in_=xf_d[s]), 16)
                    S_(f)
                    dm = qS.mark

                    st = ln_stats(
                        xs.rearrange("p (c w) -> p c w", c=NCH),
                        bO, varA[0:1, par * 512:(par + 1) * 512], xb16, xq16,
                        m1[0:1, s * 512:(s + 1) * 512],
                        rstd1[0:1, s * 512:(s + 1) * 512],
                        waits=[(qS, dm)],
                        war_convert=((PE, statA[s - 1]) if s >= 1
                                     else None))
                    statA[s] = st["conv_pe"]

                    mb = mbcA[:, par * 512:(par + 1) * 512]
                    rb = rbcA[:, par * 512:(par + 1) * 512]

                    def f(a, st=st, mb=mb, rb=rb, s=s):
                        a.wait_ge(PE.h, st["bc_pe"])
                        if s >= 2:
                            vm, pm = applyA[s - 2]
                            a.wait_ge(VE.h, vm)
                            a.wait_ge(PO.h, pm)
                        a.activation(mb, st["mb"], AF.Copy)
                        AC.inc(a.activation(rb, st["rb"], AF.Copy))
                    A_(f)
                    em = AC.n

                    CV = 13
                    ln1b_v = ln1b[:].rearrange("p (c w) -> p c w", c=NCH)
                    xs_v = xs.rearrange("p (c w) -> p c w", c=NCH)

                    def fv(v, s=s, mb=mb, rb=rb, em=em, xs_v=xs_v):
                        v.wait_ge(AC.h, em)
                        xw = xs_v[:, 0:CV, :]
                        v.tensor_tensor(
                            xw, xw,
                            mb.unsqueeze(1).broadcast_to([128, CV, 512]),
                            ALU.subtract)
                        VE.inc(v.tensor_tensor(
                            ln1b_v[:, 0:CV, s * 512:(s + 1) * 512], xw,
                            rb.unsqueeze(1).broadcast_to([128, CV, 512]),
                            ALU.mult))
                    V_(fv)

                    def fp(p, s=s, mb=mb, rb=rb, em=em, xs_v=xs_v):
                        p.wait_ge(AC.h, em)
                        nc_ = NCH - CV
                        xw = xs_v[:, CV:NCH, :]
                        p.tensor_tensor(
                            xw, xw,
                            mb.unsqueeze(1).broadcast_to([128, nc_, 512]),
                            ALU.subtract)
                        PO.inc(p.tensor_tensor(
                            ln1b_v[:, CV:NCH, s * 512:(s + 1) * 512], xw,
                            rb.unsqueeze(1).broadcast_to([128, nc_, 512]),
                            ALU.mult))
                    P_(fp)
                    applyA[s] = (VE.n, PO.n)

            with (
                nc.sbuf_tensor("wsl", [128, 3 * NCH * 128], BF16) as wsl,
                nc.sbuf_tensor("ropeT", [128, 2 * S], F32) as ropeT,
                nc.sbuf_tensor("kv1", [128, S], F32) as kv1,
                nc.sbuf_tensor("kv2", [128, S], F32) as kv2,
                nc.sbuf_tensor("kv3", [128, S], F32) as kv3,
                nc.sbuf_tensor("kv4", [128, S], F32) as kv4,
                nc.sbuf_tensor("rt1", [128, S], F32) as rt1,
                nc.sbuf_tensor("rt2", [128, S], F32) as rt2,
                nc.sbuf_tensor("cB1", [128, S], F32) as cB1,
                nc.sbuf_tensor("cB2", [128, S], F32) as cB2,
                nc.sbuf_tensor("cB3", [128, S], F32) as cB3,
                nc.sbuf_tensor("cB4", [128, S], F32) as cB4,
            ):
                jA = [(VE, applyA[2][0]), (PO, applyA[2][1]),
                      (VE, applyA[3][0]), (PO, applyA[3][1]),
                      (PE, statA[3]), (AC, AC.n)]

                def f(g):
                    for ct, val in jA:
                        qwait(g, ct, val)
                P_(f)

                def f(sy):
                    for ct, val in jA:
                        qwait(sy, ct, val)
                S_(f)
                kv_tiles = {"kv1": kv1, "kv2": kv2, "kv3": kv3, "kv4": kv4}
                tile_free = {k: None for k in kv_tiles}
                slot_free = [None] * 6
                slot_rr = [0]
                bank_free = [[None] * 4 for _ in range(2)]
                chunk_ct = [0]
                firstB = [True]

                def kv_chunk_and_evac(tname, col):
                    slot = slot_rr[0] % 3
                    slot_rr[0] += 1

                    def f(g, slot=slot, col=col):
                        if slot_free[slot] is not None:
                            ct, val = slot_free[slot]
                            qwait(g, ct, val)
                        dst = wsl[:, slot * NCH * 128:
                                  (slot + 1) * NCH * 128]
                        qG.chain(g)
                        qG.inc(g.dma_start(out=dst, in_=wkv_d[col]), 16)
                    P_(f)
                    ld = qG.mark
                    par = chunk_ct[0] % 2
                    chunk_ct[0] += 1
                    bO = PS[par]

                    ts_pe = [0] * 4

                    def f(t, slot=slot, ld=ld, bO=bO, par=par, ts_pe=ts_pe):
                        if firstB[0]:
                            for ct, val in jA:
                                qwait(t, ct, val)
                            firstB[0] = False
                        qwait(t, qG, ld)
                        for ts in range(4):
                            if bank_free[par][ts] is not None:
                                t.wait_ge(AC.h, bank_free[par][ts])
                            for ch in range(NCH):
                                inst = t.matmul(
                                    bO[:, ts * 512:(ts + 1) * 512],
                                    wsl[:, slot * NCH * 128 + ch * 128:
                                        slot * NCH * 128 + (ch + 1) * 128],
                                    ln1b[:, ch * S + ts * 512:
                                         ch * S + (ts + 1) * 512],
                                    start=(ch == 0), stop=(ch == NCH - 1))
                            ts_pe[ts] = PE.inc(inst)
                    T_(f)
                    pm = PE.n
                    slot_free[slot] = (PE, pm)
                    dst = kv_tiles[tname]
                    tf = tile_free[tname]

                    def f(a, bO=bO, dst=dst, col=col, ts_pe=ts_pe, tf=tf,
                          par=par):
                        if tf is not None:
                            a.wait_ge(tf[0].h, tf[1])
                        for ts in range(4):
                            a.wait_ge(PE.h, ts_pe[ts])
                            bank_free[par][ts] = AC.inc(a.activation(
                                dst[:, ts * 512:(ts + 1) * 512],
                                bO[:, ts * 512:(ts + 1) * 512],
                                AF.Identity, bias=btb[:, col:col + 1]))
                    A_(f)
                    return AC.n

                cos_prev_ac = {"v": 0, "p": 0}

                def emit_cos(src, waits, eng="v"):
                    if eng == "v":
                        E, s1_, s2_, sect = VE, cB1, cB2, V_
                    else:
                        E, s1_, s2_, sect = PO, cB3, cB4, P_

                    def f(x, src=src, waits=waits, pv=cos_prev_ac[eng],
                          E=E, s1_=s1_, s2_=s2_, eng=eng):
                        for ct, val in waits:
                            qwait(x, ct, val)
                        if eng == "v":
                            E.inc(x.tensor_scalar(s1_[:, :], src[:, :],
                                                  INV2PI, MAGIC,
                                                  ALU.mult, ALU.add))
                            x.wait_ge(E.h, E.n)
                            E.inc(x.tensor_scalar(s1_[:, :], s1_[:, :],
                                                  MAGIC, None,
                                                  ALU.subtract))
                            x.wait_ge(E.h, E.n)
                            x.wait_ge(AC.h, pv)
                            E.inc(x.scalar_tensor_tensor(
                                s2_[:, :], src[:, :], INV2PI, s1_[:, :],
                                ALU.mult, ALU.subtract))
                        else:
                            x.wait_ge(AC.h, pv)
                            cb = c2pi[:, 0:1].broadcast_to([128, S])
                            mb_ = cmag[:, 0:1].broadcast_to([128, S])
                            x.tensor_tensor(s1_[:, :], src[:, :], cb,
                                            ALU.mult)
                            x.tensor_tensor(s2_[:, :], s1_[:, :], mb_,
                                            ALU.add)
                            x.tensor_tensor(s2_[:, :], s2_[:, :], mb_,
                                            ALU.subtract)
                            E.inc(x.tensor_tensor(s2_[:, :], s1_[:, :],
                                                  s2_[:, :], ALU.subtract))
                    sect(f)
                    vm = E.n

                    def f(a, src=src, vm=vm, E=E, s2_=s2_):
                        a.wait_ge(E.h, vm)
                        AC.inc(a.activation(s2_[:, :], s2_[:, :], AF.Abs))
                        a.wait_ge(AC.h, AC.n)
                        AC.inc(a.activation(src[:, :], s2_[:, :], AF.Sin,
                                            bias=f32i(PIH),
                                            scale=f32i(-TWOPI)))
                    A_(f)
                    cos_prev_ac[eng] = AC.n
                    return AC.n

                rope_mark = [0]
                rope_free = [None]

                def rotary(lo, hi, waits):
                    def f(v, lo=lo, hi=hi, waits=waits):
                        for ct, val in waits:
                            qwait(v, ct, val)
                        qwait(v, qS, rope_mark[0])
                        v.tensor_tensor(rt1[:, :], lo[:, :], ropeT[:, 0:S],
                                        ALU.mult)
                        v.tensor_tensor(rt2[:, :], hi[:, :], ropeT[:, S:2 * S],
                                        ALU.mult)
                        v.tensor_tensor(rt1[:, :], rt1[:, :], rt2[:, :],
                                        ALU.subtract)
                        v.tensor_tensor(rt2[:, :], hi[:, :],
                                        ropeT[:, 0:S], ALU.mult)
                        v.tensor_tensor(cB3[:, :], lo[:, :],
                                        ropeT[:, S:2 * S], ALU.mult)
                        v.tensor_tensor(hi[:, :], rt2[:, :], cB3[:, :],
                                        ALU.add)
                        VE.inc(v.tensor_copy(lo[:, :], rt1[:, :]))
                    V_(f)
                    return VE.n, VE.n

                groups = []
                pj = 8
                for c in range(4):
                    groups.append(("quad", c))
                    groups.append(("plain", pj))
                    groups.append(("plain", pj + 1))
                    pj += 2
                plain_ct = [0]
                for kind, c in groups:
                    if kind == "quad":
                        def f(sy, c=c):
                            if rope_free[0] is not None:
                                sy.wait_ge(rope_free[0][0].h,
                                           rope_free[0][1])
                            qS.chain(sy)
                            qS.inc(sy.dma_start(out=ropeT[:, :],
                                                in_=rope_d[c]), 16)
                        S_(f)
                        rope_mark[0] = qS.mark

                        kv_chunk_and_evac("kv1", c)
                        em2 = kv_chunk_and_evac("kv2", c + 4)
                        kv_chunk_and_evac("kv3", 16 + c)
                        em4 = kv_chunk_and_evac("kv4", 16 + c + 4)

                        vK, pK = rotary(kv1, kv2, [(AC, em2)])
                        c2 = emit_cos(kv2, [(VE, pK)], eng="v")
                        c1 = emit_cos(kv1, [(VE, vK)], eng="v")
                        vV, pV = rotary(kv3, kv4, [(AC, em4), (AC, c1),
                                                   (AC, c2)])
                        c4 = emit_cos(kv4, [(VE, pV)], eng="v")
                        c3 = emit_cos(kv3, [(VE, vV)], eng="v")
                        rope_free[0] = (VE, pV)

                        def f(v, c=c, m=max(c3, c4, c1, c2)):
                            v.wait_ge(AC.h, m)
                            VE.inc(v.tensor_tensor(cB1[:, :], kv1[:, :],
                                                   kv3[:, :], ALU.mult))
                            v.wait_ge(VE.h, VE.n)
                            VE.inc(v.reduce_sum(A_acc[:, c:c + 1], cB1[:, :],
                                                AX.X))
                            VE.inc(v.tensor_tensor(cB2[:, :], kv2[:, :],
                                                   kv4[:, :], ALU.mult))
                            v.wait_ge(VE.h, VE.n)
                            VE.inc(v.reduce_sum(A_acc[:, c + 4:c + 5],
                                                cB2[:, :], AX.X))
                        V_(f)
                        fm = (VE, VE.n)
                        for k in kv_tiles:
                            tile_free[k] = fm
                    else:
                        if plain_ct[0] % 2 == 0:
                            tK, tV = "kv1", "kv3"
                        else:
                            tK, tV = "kv2", "kv4"
                        plain_ct[0] += 1
                        emK = kv_chunk_and_evac(tK, c)
                        emV = kv_chunk_and_evac(tV, 16 + c)
                        cK = emit_cos(kv_tiles[tK], [(AC, emK)], eng="v")
                        cV = emit_cos(kv_tiles[tV], [(AC, emV)], eng="v")

                        def f(v, j=c, m=max(cK, cV), ta=kv_tiles[tK],
                              tb=kv_tiles[tV]):
                            v.wait_ge(AC.h, m)
                            VE.inc(v.tensor_tensor(cB1[:, :], ta[:, :],
                                                   tb[:, :], ALU.mult))
                            v.wait_ge(VE.h, VE.n)
                            VE.inc(v.reduce_sum(A_acc[:, j:j + 1],
                                                cB1[:, :], AX.X))
                        V_(f)
                        fm = (VE, VE.n)
                        tile_free[tK] = fm
                        tile_free[tV] = fm

                def f(v):
                    v.wait_ge(VE.h, VE.n)
                    VE.inc(v.tensor_scalar(A_acc[:, :], A_acc[:, :],
                                           0.5, 0.5, ALU.mult, ALU.add))
                V_(f)
                A_ready = VE.n
                if DEBUG_IO[0]:
                    def f(a, ar=A_ready):
                        a.wait_ge(VE.h, ar)
                        qA.inc(a.dma_start(out=dbgA_d[:, :],
                                           in_=A_acc[:, :]), 16)
                        qA.inc(a.dma_start(out=dbgM_d[0:1, :],
                                           in_=m1[0:1, :]), 16)
                        qA.inc(a.dma_start(out=dbgM_d[1:2, :],
                                           in_=rstd1[0:1, :]), 16)
                    A_(f)
                endB = [(VE, VE.n), (PO, PO.n), (AC, AC.n), (PE, PE.n),
                        (qS, qS.snap), (qG, qG.snap)]

        with (
            nc.sbuf_tensor("xsubC", [128, 2 * NCH * 512], F32) as xsub,
            nc.sbuf_tensor("ln1f", [128, NCH * 512], F32) as ln1f,
            nc.sbuf_tensor("z16", [128, NCH * 512], BF16) as z16,
            nc.sbuf_tensor("xb16C", [128, NCH * 512], BF16) as xb16,
            nc.sbuf_tensor("xq16C", [128, NCH * 512], BF16) as xq16,
            nc.sbuf_tensor("mbcC", [128, 2 * 512], F32) as mbcC,
            nc.sbuf_tensor("rbcC", [128, 2 * 512], F32) as rbcC,
            nc.sbuf_tensor("logits", [8, S], F32) as logits,
            nc.sbuf_tensor("rsm", [128, 1392], F32) as rsm,
            nc.sbuf_tensor("varC", [1, 1024], F32) as varC,
            nc.sbuf_tensor("mC", [1, 1024], F32) as mC,
            nc.sbuf_tensor("rC", [1, 1024], F32) as rC,
        ):
            z_store = [None] * NSUB
            x1_store = [None] * NSUB
            sub_done = [None] * NSUB
            logit_marks = [None] * NSUB
            logit_marks_pe = [None] * NSUB
            stat_war = [None]
            firstC = [True]

            for s in range(NSUB):
                par = s % 2
                xs = xsub[:, par * NCH * 512:(par + 1) * NCH * 512]
                bO = PS[par]
                mb = mbcC[:, par * 512:(par + 1) * 512]
                rb = rbcC[:, par * 512:(par + 1) * 512]

                def f(sy, s=s, xs=xs):
                    if s == 0:
                        for ct, val in endB:
                            qwait(sy, ct, val)
                    if s >= 2:
                        vm, pm = sub_done[s - 2]
                        sy.wait_ge(VE.h, vm)
                        sy.wait_ge(PO.h, pm)
                        qwait(sy, qA, x1_store[s - 2])
                    qS.chain(sy)
                    qS.inc(sy.dma_start(out=xs[:, :], in_=xf_d[s]), 16)
                S_(f)
                dm = qS.mark

                def f(t, s=s, bO=bO):
                    if firstC[0]:
                        for ct, val in endB:
                            qwait(t, ct, val)
                        firstC[0] = False
                    if s >= 2:
                        t.wait_ge(VE.h, logit_marks[s - 2])
                    t.matmul(bO[:, 0:512], ones_row[:, :],
                             m1[0:1, s * 512:(s + 1) * 512],
                             start=True, stop=True)
                    PE.inc(t.matmul(bO[:, 512:1024], ones_row[:, :],
                                    rstd1[0:1, s * 512:(s + 1) * 512],
                                    start=True, stop=True))
                T_(f)
                bc1 = PE.n

                def f(a, bO=bO, bc1=bc1, s=s, mb=mb, rb=rb):
                    a.wait_ge(PE.h, bc1)
                    if s >= 2:
                        vm, pm = sub_done[s - 2]
                        a.wait_ge(VE.h, vm)
                        a.wait_ge(PO.h, pm)
                    a.activation(mb, bO[:, 0:512], AF.Copy)
                    AC.inc(a.activation(rb, bO[:, 512:1024], AF.Copy))
                A_(f)
                em1 = AC.n

                CV = 13
                xs_v = xs.rearrange("p (c w) -> p c w", c=NCH)
                ln1f_v = ln1f[:].rearrange("p (c w) -> p c w", c=NCH)
                z16_v = z16[:].rearrange("p (c w) -> p c w", c=NCH)

                def mbc3(bc, n):
                    return bc.unsqueeze(1).broadcast_to([128, n, 512])

                def fv(v, xs_v=xs_v, dm=dm, em1=em1, mb=mb, rb=rb):
                    qwait(v, qS, dm)
                    v.wait_ge(AC.h, em1)
                    v.wait_ge(VE.h, A_ready)
                    d = ln1f_v[:, 0:CV, :]
                    v.tensor_tensor(d, xs_v[:, 0:CV, :], mbc3(mb, CV),
                                    ALU.subtract)
                    v.tensor_tensor(d, d, mbc3(rb, CV), ALU.mult)
                    VE.inc(v.tensor_tensor(
                        d, d, A_acc[:, 0:CV].unsqueeze(2).broadcast_to(
                            [128, CV, 512]), ALU.mult))
                V_(fv)

                def fp(p, xs_v=xs_v, dm=dm, em1=em1, mb=mb, rb=rb):
                    qwait(p, qS, dm)
                    p.wait_ge(AC.h, em1)
                    p.wait_ge(VE.h, A_ready)
                    nc_ = NCH - CV
                    d = ln1f_v[:, CV:NCH, :]
                    p.tensor_tensor(d, xs_v[:, CV:NCH, :], mbc3(mb, nc_),
                                    ALU.subtract)
                    p.tensor_tensor(d, d, mbc3(rb, nc_), ALU.mult)
                    PO.inc(p.tensor_tensor(
                        d, d, A_acc[:, CV:NCH].unsqueeze(2).broadcast_to(
                            [128, nc_, 512]), ALU.mult))
                P_(fp)
                t3_marks = [(VE, VE.n), (PO, PO.n)]

                st2 = ln_stats(
                    ln1f_v,
                    bO, varC[0:1, par * 512:(par + 1) * 512], xb16, xq16,
                    mC[0:1, par * 512:(par + 1) * 512],
                    rC[0:1, par * 512:(par + 1) * 512],
                    waits=t3_marks,
                    war_convert=((PE, stat_war[0]) if stat_war[0]
                                 is not None else None))
                stat_war[0] = st2["conv_pe"]

                def f(a, st2=st2, mb=mb, rb=rb):
                    a.wait_ge(PE.h, st2["bc_pe"])
                    a.activation(mb, st2["mb"], AF.Copy)
                    AC.inc(a.activation(rb, st2["rb"], AF.Copy))
                A_(f)
                em2 = AC.n

                def fv(v, xs_v=xs_v, em2=em2, mb=mb, rb=rb):
                    v.wait_ge(AC.h, em2)
                    d = ln1f_v[:, 0:CV, :]
                    v.tensor_tensor(d, d, mbc3(mb, CV), ALU.subtract)
                    v.tensor_tensor(d, d, mbc3(rb, CV), ALU.mult)
                    VE.inc(v.tensor_tensor(xs_v[:, 0:CV, :],
                                           xs_v[:, 0:CV, :], d, ALU.add))
                V_(fv)

                def fp(p, xs_v=xs_v, em2=em2, mb=mb, rb=rb):
                    p.wait_ge(AC.h, em2)
                    nc_ = NCH - CV
                    d = ln1f_v[:, CV:NCH, :]
                    p.tensor_tensor(d, d, mbc3(mb, nc_), ALU.subtract)
                    p.tensor_tensor(d, d, mbc3(rb, nc_), ALU.mult)
                    PO.inc(p.tensor_tensor(xs_v[:, CV:NCH, :],
                                           xs_v[:, CV:NCH, :], d, ALU.add))
                P_(fp)
                x1_marks = [(VE, VE.n), (PO, PO.n)]

                def f(a, s=s, xs=xs, x1m=x1_marks):
                    for ct, val in x1m:
                        qwait(a, ct, val)
                    qA.chain(a)
                    qA.inc(a.dma_start(
                        out=x1_dram[:, s * 512:(s + 1) * 512].rearrange(
                            "(c p) w -> p c w", p=128),
                        in_=xs.rearrange("p (c w) -> p c w", c=NCH)), 16)
                A_(f)
                x1_store[s] = qA.mark

                st3 = ln_stats(
                    xs_v,
                    bO, varC[0:1, par * 512:(par + 1) * 512], xb16, xq16,
                    mC[0:1, par * 512:(par + 1) * 512],
                    rC[0:1, par * 512:(par + 1) * 512],
                    waits=x1_marks, war_convert=(PE, stat_war[0]))
                stat_war[0] = st3["conv_pe"]

                def f(a, st3=st3, mb=mb, rb=rb):
                    a.wait_ge(PE.h, st3["bc_pe"])
                    a.activation(mb, st3["mb"], AF.Copy)
                    AC.inc(a.activation(rb, st3["rb"], AF.Copy))
                A_(f)
                em3 = AC.n

                def fv(v, xs_v=xs_v, em3=em3, s=s, mb=mb, rb=rb):
                    v.wait_ge(AC.h, em3)
                    if s >= 1:
                        qwait(v, qA, z_store[s - 1])
                    d = ln1f_v[:, 0:CV, :]
                    v.tensor_tensor(d, xs_v[:, 0:CV, :], mbc3(mb, CV),
                                    ALU.subtract)
                    VE.inc(v.tensor_tensor(z16_v[:, 0:CV, :], d,
                                           mbc3(rb, CV), ALU.mult))
                V_(fv)

                def fp(p, xs_v=xs_v, em3=em3, s=s, mb=mb, rb=rb):
                    p.wait_ge(AC.h, em3)
                    if s >= 1:
                        qwait(p, qA, z_store[s - 1])
                    nc_ = NCH - CV
                    d = ln1f_v[:, CV:NCH, :]
                    p.tensor_tensor(d, xs_v[:, CV:NCH, :], mbc3(mb, nc_),
                                    ALU.subtract)
                    PO.inc(p.tensor_tensor(z16_v[:, CV:NCH, :], d,
                                           mbc3(rb, nc_), ALU.mult))
                P_(fp)
                sub_done[s] = (VE.n, PO.n)
                zm = sub_done[s]

                def f(a, s=s, zm=zm):
                    a.wait_ge(VE.h, zm[0])
                    a.wait_ge(PO.h, zm[1])
                    qA.chain(a)
                    qA.inc(a.dma_start(
                        out=z_dram[:, s * 512:(s + 1) * 512].rearrange(
                            "(c p) w -> p c w", p=128),
                        in_=z16[:].rearrange("p (c w) -> p c w", c=NCH)),
                        16)
                A_(f)
                z_store[s] = qA.mark

                def f(t, s=s, bO=bO, zm=zm, em3=em3):
                    t.wait_ge(VE.h, zm[0])
                    t.wait_ge(PO.h, zm[1])
                    t.wait_ge(AC.h, em3)
                    qwait(t, qG, wrb_ld)
                    for ch in range(NCH):
                        inst = t.matmul(
                            bO[0:8, 0:512], wrb[:, ch * 8:(ch + 1) * 8],
                            z16[:, ch * 512:(ch + 1) * 512],
                            start=(ch == 0), stop=(ch == NCH - 1))
                    PE.inc(inst)
                T_(f)
                lg = PE.n
                logit_marks_pe[s] = PE.n

                def f(v, s=s, bO=bO, lg=lg):
                    v.wait_ge(PE.h, lg)
                    VE.inc(v.tensor_copy(
                        logits[0:8, s * 512:(s + 1) * 512],
                        bO[0:8, 0:512]))
                V_(f)
                logit_marks[s] = VE.n

            col = {}
            for i, nm in enumerate(["lgT", "inh", "dist", "iz", "w",
                                    "fb", "mask", "fbw", "wf", "t1"]):
                col[nm] = rsm[:, i * 128:(i + 1) * 128]
            sc = {}
            for i, nm in enumerate(["wsum", "noex", "m1r", "cnt1", "m2r",
                                    "fs", "ws"]):
                sc[nm] = rsm[:, 1280 + i * 16:1280 + (i + 1) * 16]

            def c3(nm):
                return col[nm].rearrange("p (c e) -> p c e", c=NCH)

            def s3(nm):
                return sc[nm].unsqueeze(2).broadcast_to([128, NCH, 8])

            tp_marks = [None, None]
            cp_marks = [None, None]
            for k in range(NCH):
                kp = k % 2
                base = 1536 if kp == 0 else 1024
                ps_t = psB[0:128, base:base + 8]

                def f(t, k=k, ps_t=ps_t, w=(cp_marks[k % 2]
                                            if k >= 2 else None)):
                    if k == 0:
                        t.wait_ge(VE.h, logit_marks[NSUB - 1])
                    if w is not None:
                        t.wait_ge(VE.h, w)
                    PE.inc(t.transpose(
                        ps_t, logits[0:8, k * 128:(k + 1) * 128],
                        id8[:, :]))
                T_(f)
                tp_marks[kp] = PE.n

                def f(v, k=k, ps_t=ps_t, tp=PE.n):
                    v.wait_ge(PE.h, tp)
                    VE.inc(v.tensor_copy(col["lgT"][:, k * 8:(k + 1) * 8],
                                         ps_t))
                V_(f)
                cp_marks[kp] = VE.n
            gather_done = VE.n

            def f(a, gd=gather_done):
                a.wait_ge(VE.h, gd)
                a.wait_ge(VE.h, invT_mark)
                AC.inc(a.activation(col["inh"], col["lgT"], AF.Sigmoid,
                                    scale=invT_bc[:, 0:1]))
                a.wait_ge(AC.h, AC.n)
                AC.inc(a.activation(col["dist"], col["inh"], AF.Abs,
                                    bias=f32i(-GC)))
                a.wait_ge(AC.h, AC.n)
                AC.inc(a.activation(col["w"], col["dist"], AF.Exp,
                                    scale=f32i(-10.0)))
                AC.inc(a.activation(col["fb"], col["dist"], AF.Exp,
                                    scale=f32i(-1.0 / 0.3)))
            A_(f)
            am = AC.n

            def f(v, am=am):
                v.wait_ge(AC.h, am)
                steps = [
                    lambda: v.tensor_scalar(col["iz"], col["inh"], GL,
                                            None, ALU.is_ge),
                    lambda: v.tensor_scalar(col["t1"], col["inh"], GU,
                                            None, ALU.is_le),
                    lambda: v.tensor_tensor(col["iz"], col["iz"],
                                            col["t1"], ALU.mult),
                    lambda: v.tensor_tensor(col["w"], col["w"], col["iz"],
                                            ALU.mult),
                    lambda: v.reduce_sum(sc["wsum"], c3("w"), AX.X),
                    lambda: v.reduce_max(sc["m1r"], c3("fb"), AX.X),
                    lambda: v.tensor_scalar(sc["noex"], sc["wsum"], 1e-8,
                                            None, ALU.is_lt),
                    lambda: v.tensor_tensor(c3("mask"), c3("fb"),
                                            s3("m1r"), ALU.is_ge),
                    lambda: v.reduce_sum(sc["cnt1"], c3("mask"), AX.X),
                    lambda: v.scalar_tensor_tensor(col["t1"], col["mask"],
                                                   -1e30, col["fb"],
                                                   ALU.mult, ALU.add),
                    lambda: v.reduce_max(sc["m2r"], c3("t1"), AX.X),
                    lambda: v.tensor_tensor(c3("t1"), c3("t1"),
                                            s3("m2r"), ALU.is_ge),
                    lambda: v.tensor_scalar(sc["cnt1"], sc["cnt1"], 1.5,
                                            None, ALU.is_lt),
                    lambda: v.tensor_tensor(c3("t1"), c3("t1"),
                                            s3("cnt1"), ALU.mult),
                    lambda: v.tensor_tensor(col["mask"], col["mask"],
                                            col["t1"], ALU.add),
                    lambda: v.tensor_tensor(col["fbw"], col["fb"],
                                            col["mask"], ALU.mult),
                    lambda: v.reduce_sum(sc["fs"], c3("fbw"), AX.X),
                    lambda: v.tensor_scalar(sc["fs"], sc["fs"], 1e-8,
                                            None, ALU.max),
                    lambda: v.reciprocal(sc["fs"], sc["fs"]),
                    lambda: v.tensor_tensor(c3("fbw"), c3("fbw"),
                                            s3("fs"), ALU.mult),
                    lambda: v.tensor_tensor(c3("wf"), c3("fbw"),
                                            s3("noex"), ALU.mult),
                    lambda: v.tensor_tensor(c3("t1"), c3("w"),
                                            s3("noex"), ALU.mult),
                    lambda: v.tensor_tensor(col["t1"], col["t1"],
                                            col["w"], ALU.subtract),
                    lambda: v.tensor_tensor(col["wf"], col["wf"],
                                            col["t1"], ALU.subtract),
                    lambda: v.reduce_sum(sc["ws"], c3("wf"), AX.X),
                    lambda: v.tensor_scalar(sc["ws"], sc["ws"], 1e-8,
                                            None, ALU.max),
                    lambda: v.reciprocal(sc["ws"], sc["ws"]),
                    lambda: v.tensor_tensor(c3("wf"), c3("wf"),
                                            s3("ws"), ALU.mult),
                ]
                for emit in steps:
                    v.wait_ge(VE.h, VE.n)
                    VE.inc(emit())
            V_(f)
            math_done = VE.n

            bk_marks = [None, None]
            for k in range(NCH):
                kp = k % 2
                base = 1024 if kp == 0 else 1536
                ps_t = psB[0:8, base:base + 128]

                def f(t, k=k, ps_t=ps_t, md=math_done,
                      w=(bk_marks[k % 2] if k >= 2 else None)):
                    t.wait_ge(VE.h, md)
                    if w is not None:
                        t.wait_ge(VE.h, w)
                    PE.inc(t.transpose(ps_t,
                                       col["wf"][:, k * 8:(k + 1) * 8],
                                       id128[:, :]))
                T_(f)

                def f(v, k=k, ps_t=ps_t, tb=PE.n):
                    v.wait_ge(PE.h, tb)
                    VE.inc(v.tensor_copy(wT[0:8, k * 128:(k + 1) * 128],
                                         ps_t))
                V_(f)
                bk_marks[kp] = VE.n
            wT_ready = VE.n
            endC = [(VE, VE.n), (PO, PO.n), (AC, AC.n), (PE, PE.n),
                    (qA, qA.snap), (qS, qS.snap), (qG, qG.snap)]
            endC_core = [(VE, sub_done[NSUB - 1][0]),
                         (PO, sub_done[NSUB - 1][1]),
                         (PE, logit_marks_pe[NSUB - 1]),
                         (qA, z_store[NSUB - 1])]

    with (
        nc.sbuf_tensor("zg", [128, NCH * TG], BF16) as zg,
        nc.sbuf_tensor("yacc", [128, NCH * TG], F32) as yacc,
        nc.sbuf_tensor("midD", [128, NCH * TG], BF16) as mid,
        nc.sbuf_tensor("ringD", [128, 2 * NCH * 512], BF16) as ring,
        nc.sbuf_tensor("sgD", [128, TG], F32) as sg,
        nc.sbuf_tensor("wbcD", [128, TG], F32) as wbc,
        nc.sbuf_tensor("yscrD", [128, TG], F32) as yscr,
        nc.sbuf_tensor("hgD", [64, TG], BF16) as hg,
        nc.sbuf_tensor("huD", [64, TG], BF16) as hu,
        nc.sbuf_tensor("hdD", [64, TG], BF16) as hd,
        nc.sbuf_tensor("guaD", [128, 2 * NCH * R], BF16) as gua,
        nc.sbuf_tensor("gubD", [64, 2 * D], BF16) as gub,
        nc.sbuf_tensor("selD", [8, 1024], BF16) as sel,
    ):
        scrV = sg[:, 0:512]
        scrP = sg[:, 512:1024]
        mid_f32 = mid[:].bitcast(F32)
        hg_f = hg[:].bitcast(F32)
        hu_f = hu[:].bitcast(F32)
        hd_f = hd[:].bitcast(F32)
        gua_f = gua[:].bitcast(F32)
        gub_f = gub[:].bitcast(F32)
        epi_rows = {(0, "v"): hg_f[0:1, 0:512], (1, "v"): hu_f[0:1, 0:512],
                    (0, "m"): hd_f[0:1, 0:512], (1, "m"): gua_f[0:1, 0:512],
                    (0, "r"): gua_f[0:1, 512:1024],
                    (1, "r"): gub_f[0:1, 0:512]}
        zb16s = zg[:, 0:NCH * 512]
        zq16s = zg[:, NCH * 512:NCH * 1024]

        firstD = [True]
        prev_zg_free = [None]
        ring_free = [None, None]
        freeA = {0: None, 1: None}
        freeB = {0: None, 1: None}
        lr_free = [None]
        prev_x2_store = [None]
        prev_epi = [None]

        def gwait(g, cond):
            if cond is not None:
                qwait(g, cond[0], cond[1])

        def f(g):
            for ct, val in endC_core:
                qwait(g, ct, val)
        P_(f)
        sel_ld = [None]

        for g_i in range(NG):
            def f(sy, g_i=g_i):
                if g_i == 0:
                    for ct, val in endC_core:
                        qwait(sy, ct, val)
                else:
                    sy.wait_ge(PE.h, prev_zg_free[0])
                qS.chain(sy)
                qS.inc(sy.dma_start(
                    out=zg[:].rearrange("p (c w) -> p c w", c=NCH),
                    in_=z_dram[:, g_i * TG:(g_i + 1) * TG].rearrange(
                        "(c p) w -> p c w", p=128)), 16)
                if g_i == 0:
                    for ct, val in endC:
                        qwait(sy, ct, val)
                    qS.chain(sy)
                    qS.inc(sy.dma_start(out=sel[:, :],
                                        in_=sel_c.ap()[:, :]), 16)
                    sel_ld[0] = qS.mark
            S_(f)
            zl = qS.mark

            sg_free = [None]
            mid15 = [None]
            last_up_pe = [None]
            last_gate_pe = [None]

            for ei, e in enumerate([2, 3, 0, 1, 4, 5, 6, 7]):
                low = e < NL

                h_ac = None
                if low:
                    def f(g, e=e, lf=lr_free[0], g_i=g_i,
                          pe_epi=(prev_epi[0][3] if prev_epi[0] is not None
                                  else None)):
                        gwait(g, lf)
                        if e == 0 and g_i == 0:
                            g.wait_ge(VE.h, wT_ready)
                        if e == 0 and g_i == 1 and pe_epi is not None:
                            g.wait_ge(PE.h, pe_epi)
                        qG.chain(g)
                        qG.inc(g.dma_start(out=gua[:, :],
                                           in_=gua_d[e]), 16)
                        qG.chain(g)
                        qG.inc(g.dma_start(out=gub[:, :],
                                           in_=gub_d[e]), 16)
                    P_(f)
                    lr_ld = qG.snap

                    def f(t, zl=zl, lr=lr_ld, fb0=freeB[0],
                          fb1=freeB[1]):
                        qwait(t, qS, zl)
                        qwait(t, qG, lr)
                        gwait(t, fb0)
                        for ch in range(NCH):
                            for ts in range(2):
                                inst = t.matmul(
                                    psB[0:64, ts * 512:(ts + 1) * 512],
                                    gua[:, ch * R:(ch + 1) * R],
                                    zg[:, ch * TG + ts * 512:
                                       ch * TG + (ts + 1) * 512],
                                    start=(ch == 0),
                                    stop=(ch == NCH - 1))
                        PE.inc(inst)
                        gwait(t, fb1)
                        for ch in range(NCH):
                            for ts in range(2):
                                inst = t.matmul(
                                    psB[0:64, 1024 + ts * 512:
                                        1024 + (ts + 1) * 512],
                                    gua[:, NCH * R + ch * R:
                                        NCH * R + (ch + 1) * R],
                                    zg[:, ch * TG + ts * 512:
                                       ch * TG + (ts + 1) * 512],
                                    start=(ch == 0),
                                    stop=(ch == NCH - 1))
                        PE.inc(inst)
                    T_(f)
                    h_pe = PE.n

                    def f(a, h_pe=h_pe):
                        a.wait_ge(PE.h, h_pe)
                        a.activation(hg[:, :], psB[0:64, 0:1024], AF.Copy)
                        AC.inc(a.activation(hu[:, :],
                                            psB[0:64, 1024:2048],
                                            AF.Copy))
                    A_(f)
                    h_ac = AC.n
                    freeB[0] = (AC, h_ac)
                    freeB[1] = (AC, h_ac)

                for fch in range(NCH):
                    slot = fch % 2
                    sbase = slot * NCH * 256

                    def f(g, sbase=sbase, e=e, fch=fch,
                          rf=ring_free[slot]):
                        gwait(g, rf)
                        qG.chain(g)
                        qG.inc(g.dma_start(
                            out=ring[:, sbase:sbase + NCH * 256],
                            in_=wgu_d[e, fch]), 16)
                    P_(f)
                    ld = qG.mark

                    par = fch % 2
                    gb_ps = psA[:, par * 1024:(par + 1) * 1024]
                    ub_ps = psB[:, par * 1024:(par + 1) * 1024]

                    def fg(t, ld=ld, zl=zl, sbase=sbase, fch=fch,
                           gb_ps=gb_ps, low=low, h_ac=h_ac,
                           fa=freeA[par]):
                        qwait(t, qG, ld)
                        qwait(t, qS, zl)
                        gwait(t, fa)
                        if low:
                            t.wait_ge(AC.h, h_ac)
                        inst = None
                        for ch in range(NCH):
                            for ts in range(2):
                                inst = t.matmul(
                                    gb_ps[:, ts * 512:(ts + 1) * 512],
                                    ring[:, sbase + ch * 128:
                                         sbase + (ch + 1) * 128],
                                    zg[:, ch * TG + ts * 512:
                                       ch * TG + (ts + 1) * 512],
                                    start=(ch == 0),
                                    stop=(ch == NCH - 1 and not low))
                        if low:
                            for ts in range(2):
                                inst = t.matmul(
                                    gb_ps[:, ts * 512:(ts + 1) * 512],
                                    gub[0:64, fch * 128:(fch + 1) * 128],
                                    hg[0:64, ts * 512:(ts + 1) * 512],
                                    start=False, stop=True)
                        PE.inc(inst)
                    T_(fg)
                    gate_pe = PE.n
                    last_gate_pe[0] = gate_pe

                    def fu(t, ld=ld, sbase=sbase, fch=fch, ub_ps=ub_ps,
                           low=low, fb=freeB[par]):
                        gwait(t, fb)
                        inst = None
                        for ch in range(NCH):
                            for ts in range(2):
                                inst = t.matmul(
                                    ub_ps[:, ts * 512:(ts + 1) * 512],
                                    ring[:, sbase + NCH * 128 + ch * 128:
                                         sbase + NCH * 128 +
                                         (ch + 1) * 128],
                                    zg[:, ch * TG + ts * 512:
                                       ch * TG + (ts + 1) * 512],
                                    start=(ch == 0),
                                    stop=(ch == NCH - 1 and not low))
                        if low:
                            for ts in range(2):
                                inst = t.matmul(
                                    ub_ps[:, ts * 512:(ts + 1) * 512],
                                    gub[0:64, D + fch * 128:
                                         D + (fch + 1) * 128],
                                    hu[0:64, ts * 512:(ts + 1) * 512],
                                    start=False, stop=True)
                        PE.inc(inst)
                    T_(fu)
                    up_pe = PE.n
                    last_up_pe[0] = up_pe
                    ring_free[slot] = (PE, up_pe)

                    def fs(a, gate_pe=gate_pe, gb_ps=gb_ps,
                           sgf=sg_free[0]):
                        a.wait_ge(PE.h, gate_pe)
                        if sgf is not None:
                            a.wait_ge(VE.h, sgf)
                        AC.inc(a.activation(sg[:, :], gb_ps, AF.Silu))
                    A_(fs)
                    silu_ac = AC.n
                    freeA[par] = (AC, silu_ac)

                    def fm(v, silu_ac=silu_ac, up_pe=up_pe, ub_ps=ub_ps,
                           fch=fch, g_i=g_i, ei=ei):
                        v.wait_ge(AC.h, silu_ac)
                        v.wait_ge(PE.h, up_pe)
                        if ei == 0 and fch == 0 and g_i == 1:
                            qwait(v, qA, prev_x2_store[0])
                        VE.inc(v.tensor_tensor(
                            mid[:, fch * TG:(fch + 1) * TG],
                            sg[:, :], ub_ps, ALU.mult))
                    V_(fm)
                    mid_ve = VE.n
                    freeB[par] = (VE, mid_ve)
                    sg_free[0] = mid_ve
                    if fch == NCH - 1:
                        mid15[0] = mid_ve

                hd_ac = None
                if low:
                    def f(g, e=e):
                        g.wait_ge(PE.h, last_gate_pe[0])
                        qG.chain(g)
                        qG.inc(g.dma_start(out=gua[:, 0:NCH * R],
                                           in_=da_d[e]), 16)
                        qG.chain(g)
                        qG.inc(g.dma_start(out=gub[:, 0:D],
                                           in_=db_d[e]), 16)
                    P_(f)
                    dadb_ld = qG.snap

                    def f(t, dadb=dadb_ld, m15=mid15[0], fb=freeB[0]):
                        qwait(t, qG, dadb)
                        t.wait_ge(VE.h, m15)
                        gwait(t, fb)
                        inst = None
                        for ch in range(NCH):
                            for ts in range(2):
                                inst = t.matmul(
                                    psB[0:64, ts * 512:(ts + 1) * 512],
                                    gua[:, ch * R:(ch + 1) * R],
                                    mid[:, ch * TG + ts * 512:
                                        ch * TG + (ts + 1) * 512],
                                    start=(ch == 0),
                                    stop=(ch == NCH - 1))
                        PE.inc(inst)
                    T_(f)
                    hd_pe = PE.n

                    def f(a, hd_pe=hd_pe):
                        a.wait_ge(PE.h, hd_pe)
                        AC.inc(a.activation(hd[:, :], psB[0:64, 0:1024],
                                            AF.Copy))
                    A_(f)
                    hd_ac = AC.n
                    freeB[0] = (AC, hd_ac)
                    lr_free[0] = (PE, hd_pe)

                def f(t, e=e, g_i=g_i, fb1=freeB[1]):
                    qwait(t, qS, sel_ld[0])
                    t.wait_ge(VE.h, wT_ready)
                    gwait(t, fb1)
                    inst = None
                    for ts in range(2):
                        inst = t.matmul(
                            psB[:, 1024 + ts * 512:1024 + (ts + 1) * 512],
                            sel[:, e * 128:(e + 1) * 128],
                            wT[0:8, g_i * TG + ts * 512:
                               g_i * TG + (ts + 1) * 512],
                            start=True, stop=True)
                    PE.inc(inst)
                T_(f)
                wbc_pe = PE.n

                def f(v, wbc_pe=wbc_pe):
                    v.wait_ge(PE.h, wbc_pe)
                    VE.inc(v.tensor_copy(wbc[:, :], psB[:, 1024:2048]))
                V_(f)
                wbc_ve = VE.n
                freeB[1] = (VE, wbc_ve)

                for dq in range(8):
                    slot = dq % 2
                    sbase = slot * NCH * 256

                    def f(g, sbase=sbase, e=e, dq=dq,
                          rf=ring_free[slot]):
                        gwait(g, rf)
                        qG.chain(g)
                        qG.inc(g.dma_start(
                            out=ring[:, sbase:sbase + NCH * 256],
                            in_=wd_d[e, dq]), 16)
                    P_(f)
                    ld = qG.mark

                    for dl in range(2):
                        dch = dq * 2 + dl
                        par = dch % 2
                        d_ps = psA[:, par * 1024:(par + 1) * 1024]

                        def f(t, ld=ld, dch=dch, dl=dl, sbase=sbase,
                              d_ps=d_ps, low=low, hd_ac=hd_ac,
                              m15=mid15[0], fa=freeA[par]):
                            qwait(t, qG, ld)
                            t.wait_ge(VE.h, m15)
                            gwait(t, fa)
                            if low:
                                t.wait_ge(AC.h, hd_ac)
                            inst = None
                            for ch in range(NCH):
                                for ts in range(2):
                                    inst = t.matmul(
                                        d_ps[:, ts * 512:(ts + 1) * 512],
                                        ring[:, sbase + ch * 256 +
                                             dl * 128:sbase + ch * 256 +
                                             dl * 128 + 128],
                                        mid[:, ch * TG + ts * 512:
                                            ch * TG + (ts + 1) * 512],
                                        start=(ch == 0),
                                        stop=(ch == NCH - 1 and not low))
                            if low:
                                for ts in range(2):
                                    inst = t.matmul(
                                        d_ps[:, ts * 512:(ts + 1) * 512],
                                        gub[0:64, dch * 128:
                                            (dch + 1) * 128],
                                        hd[0:64, ts * 512:(ts + 1) * 512],
                                        start=False, stop=True)
                            PE.inc(inst)
                        T_(f)
                        down_pe = PE.n
                        if dl == 1:
                            ring_free[slot] = (PE, down_pe)
                        if low and dq == 7 and dl == 1:
                            lr_free[0] = (PE, down_pe)

                        def f(v, down_pe=down_pe, dch=dch, d_ps=d_ps,
                              ei=ei, g_i=g_i):
                            v.wait_ge(PE.h, down_pe)
                            if ei == 0 and dch == 0 and prev_epi[0] \
                                    is not None:
                                v.wait_ge(AC.h, prev_epi[0][0])
                                v.wait_ge(VE.h, prev_epi[0][1])
                            dst = yacc[:, dch * TG:(dch + 1) * TG]
                            if ei == 0:
                                inst = v.tensor_tensor(dst, d_ps, wbc[:, :],
                                                       ALU.mult)
                            else:
                                v.tensor_tensor(yscr[:, :], d_ps, wbc[:, :],
                                                ALU.mult)
                                inst = v.tensor_tensor(dst, dst, yscr[:, :],
                                                       ALU.add)
                            VE.inc(inst)
                        V_(f)
                        freeA[par] = (VE, VE.n)

            prev_zg_free[0] = last_up_pe[0]

            if DEBUG_IO[0] and g_i == 0:
                def f(a):
                    a.wait_ge(VE.h, freeA[0][1])
                    a.wait_ge(VE.h, freeA[1][1])
                    a.wait_ge(VE.h, wT_ready)
                    qA.inc(a.dma_start(out=dbgW_d[:, :], in_=wT[:, :]), 16)
                    qA.chain(a)
                    qA.inc(a.dma_start(
                        out=dbgY_d.rearrange("(c p) w -> p c w", p=128),
                        in_=yacc[:].rearrange("p (c w) -> p c w", c=NCH)), 16)
                    qA.chain(a)
                    qA.inc(a.dma_start(
                        out=dbgMid_d.rearrange("(c p) w -> p c w", p=128),
                        in_=mid[:].rearrange("p (c w) -> p c w", c=NCH)), 16)
                A_(f)

            epi_ac = epi_po = epi_pe = epi_bc = None
            for sub in range(2):
                par = sub % 2
                bO = PS[par]
                scol = g_i * TG + sub * 512

                def f(sy, sub=sub, scol=scol, g_i=g_i):
                    sy.wait_ge(PE.h, ring_free[0][1])
                    sy.wait_ge(PE.h, ring_free[1][1])
                    if sub == 1 or g_i == 1:
                        qwait(sy, qA, prev_x2_store[0])
                    qS.chain(sy)
                    qS.inc(sy.dma_start(
                        out=mid_f32.rearrange("p (c w) -> p c w", c=NCH),
                        in_=x1_dram[:, scol:scol + 512].rearrange(
                            "(c p) w -> p c w", p=128)), 16)
                S_(f)
                x1l = qS.mark

                yw = [(VE, freeA[0][1]), (VE, freeA[1][1])]
                y_v = yacc[:].rearrange("p (c w) -> p c w",
                                        c=NCH)[:, :, sub * 512:
                                               (sub + 1) * 512]
                st4 = ln_stats(
                    y_v,
                    bO, epi_rows[(par, "v")], zb16s, zq16s,
                    epi_rows[(par, "m")], epi_rows[(par, "r")],
                    waits=yw + [(PE, last_up_pe[0])],
                    war_convert=((AC, prev_epi[0][0]) if prev_epi[0]
                                 is not None and sub == 0 else None),
                    bank_wait=(freeA[0] if par == 0 else freeB[0]))
                epi_pe = st4["conv_pe"]
                epi_bc = st4["bc_pe"]

                def f(a, st4=st4):
                    a.wait_ge(PE.h, st4["bc_pe"])
                    a.activation(scrV[:, :], st4["mb"], AF.Copy)
                    AC.inc(a.activation(scrP[:, :], st4["rb"], AF.Copy))
                A_(f)
                em4 = AC.n
                CV = NCH
                mid_v = mid_f32.rearrange("p (c w) -> p c w", c=NCH)

                def mbc3(bc, n):
                    return bc.unsqueeze(1).broadcast_to([128, n, 512])

                def fv(v, y_v=y_v, em4=em4, x1l=x1l):
                    v.wait_ge(AC.h, em4)
                    qwait(v, qS, x1l)
                    d = y_v[:, 0:CV, :]
                    v.tensor_tensor(d, d, mbc3(scrV[:, :], CV),
                                    ALU.subtract)
                    v.tensor_tensor(d, d, mbc3(scrP[:, :], CV), ALU.mult)
                    VE.inc(v.tensor_tensor(mid_v[:, 0:CV, :],
                                           mid_v[:, 0:CV, :], d, ALU.add))
                V_(fv)
                vmark = VE.n

                if par == 0:
                    freeA[0] = (VE, vmark)
                else:
                    freeB[0] = (VE, vmark)
                epi_po = vmark
                epi_ac = AC.n

                def f(a, scol=scol, vm=vmark):
                    a.wait_ge(VE.h, vm)
                    qA.chain(a)
                    qA.inc(a.dma_start(
                        out=out_d[:, scol:scol + 512].rearrange(
                            "(c p) w -> p c w", p=128),
                        in_=mid_f32.rearrange("p (c w) -> p c w",
                                              c=NCH)), 16)
                A_(f)
                prev_x2_store[0] = qA.mark
            prev_epi[0] = (epi_ac, epi_po, epi_pe, epi_bc)

        def f(sy):
            qwait(sy, qA, qA.snap)
        S_(f)

    _es0.close()
    return nc


_ORDER = np.concatenate([np.arange(0, 1024, 2), np.arange(1, 1024, 2),
                         np.arange(1024, 2048)])
_INV = np.empty(D, np.int64)
_INV[_ORDER] = np.arange(D)

_NC_CACHE = [None]


def kernel(L0_attn_X, L0_attn_W, L0_attn_b, L0_ffn_Wg, L0_ffn_Wu, L0_ffn_Wd,
           L0_ffn_Ga, L0_ffn_Gb, L0_ffn_Ua, L0_ffn_Ub, L0_ffn_Da, L0_ffn_Db,
           L0_ffn_Wr, L0_ffn_temperature):
    X = np.asarray(L0_attn_X, np.float32)
    W = np.asarray(L0_attn_W, np.float32)
    b = np.asarray(L0_attn_b, np.float32)
    Wg = np.asarray(L0_ffn_Wg, np.float32)
    Wu = np.asarray(L0_ffn_Wu, np.float32)
    Wd = np.asarray(L0_ffn_Wd, np.float32)
    Ga = np.asarray(L0_ffn_Ga, np.float32)
    Gb = np.asarray(L0_ffn_Gb, np.float32)
    Ua = np.asarray(L0_ffn_Ua, np.float32)
    Ub = np.asarray(L0_ffn_Ub, np.float32)
    Da = np.asarray(L0_ffn_Da, np.float32)
    Db = np.asarray(L0_ffn_Db, np.float32)
    Wr = np.asarray(L0_ffn_Wr, np.float32)
    temp = np.asarray(L0_ffn_temperature, np.float32)

    o = _ORDER
    Wp = W[o, :]
    wkv = np.ascontiguousarray(
        np.concatenate([Wp[:, 0:D][:, o], Wp[:, D:2 * D][:, o]], axis=1))
    bp = np.concatenate([b[0:D][o], b[D:2 * D][o]])
    bt = np.ascontiguousarray(bp.reshape(32, 128).T)
    wkvl = np.ascontiguousarray(
        wkv.reshape(NCH, 128, 32, 128).transpose(2, 1, 0, 3)
        .reshape(32, 128, NCH * 128))
    g5 = Wg[:, o, :].reshape(NE, NCH, 128, NCH, 128).transpose(0, 3, 2, 1, 4)
    u5 = Wu[:, o, :].reshape(NE, NCH, 128, NCH, 128).transpose(0, 3, 2, 1, 4)
    wgu = np.ascontiguousarray(np.concatenate(
        [g5.reshape(NE, NCH, 128, NCH * 128),
         u5.reshape(NE, NCH, 128, NCH * 128)], axis=3))
    wdl = np.ascontiguousarray(
        Wd[:, :, o].reshape(NE, NCH, 128, 8, 256).transpose(0, 3, 2, 1, 4)
        .reshape(NE, 8, 128, NCH * 256))
    gua = np.ascontiguousarray(np.concatenate(
        [Ga[:, o, :].reshape(NL, NCH, 128, R).transpose(0, 2, 1, 3)
         .reshape(NL, 128, NCH * R),
         Ua[:, o, :].reshape(NL, NCH, 128, R).transpose(0, 2, 1, 3)
         .reshape(NL, 128, NCH * R)], axis=2))
    gub = np.ascontiguousarray(np.concatenate([Gb, Ub], axis=2))
    da = np.ascontiguousarray(
        Da.reshape(NL, NCH, 128, R).transpose(0, 2, 1, 3)
        .reshape(NL, 128, NCH * R))
    db = np.ascontiguousarray(Db[:, :, o])
    wrl = np.ascontiguousarray(
        Wr[o, :].reshape(NCH, 128, 8).transpose(1, 0, 2)
        .reshape(128, NCH * 8))

    pos = np.arange(S, dtype=np.float32)
    inv_freq = (1.0 / (10000.0 ** (np.arange(0, 1024, 2, dtype=np.float32)
                                   / 1024.0))).astype(np.float32)
    freqs = pos[:, None] * inv_freq[None, :]
    ropec = np.cos(freqs).T.astype(np.float32)
    ropes = np.sin(freqs).T.astype(np.float32)
    rope4 = np.ascontiguousarray(np.stack(
        [np.concatenate([ropec[c * 128:(c + 1) * 128],
                         ropes[c * 128:(c + 1) * 128]], axis=1)
         for c in range(4)]))

    if _NC_CACHE[0] is None:
        _NC_CACHE[0] = build_nc()
    nc = _NC_CACHE[0]

    shared = {"wkv": wkvl, "bt": bt, "rope": rope4,
              "wgu": wgu, "wd": wdl, "gua": gua, "gub": gub,
              "da": da, "db": db, "wr": wrl,
              "temp": temp.reshape(1, 1)}
    in_maps = []
    for c in range(8):
        m = dict(shared)
        m["xf"] = np.ascontiguousarray(
            X[c].T[o, :].reshape(NCH, 128, NSUB, 512)
            .transpose(2, 1, 0, 3).reshape(NSUB, 128, NCH * 512))
        in_maps.append(m)

    res = run_bass_kernel_spmd(nc, in_maps, core_ids=list(range(8)))
    out = np.empty((8, S, D), np.float32)
    for c in range(8):
        out[c] = res.results[c]["out"][_INV, :].T
    return out


# version 19
# speedup vs baseline: 1.1061x; 1.1061x over previous
import numpy as np
from contextlib import ExitStack
import concourse.bass as bass
import concourse.mybir as mybir
from concourse.bass_utils import run_bass_kernel_spmd

F32 = mybir.dt.float32
BF16 = mybir.dt.bfloat16
AF = mybir.ActivationFunctionType
ALU = mybir.AluOpType
AX = mybir.AxisListType

D = 2048
S = 2048
NCH = 16
NSUB = 4
TG = 1024
NG = 2
NE = 8
NL = 2
R = 64

MAGIC = 12582912.0
INV2PI = float(1.0 / (2.0 * np.pi))
TWOPI = float(2.0 * np.pi)
PIH = float(np.pi / 2.0)
GC = float(1.0 / np.e)
GL = float(0.5 - np.log(4.0 / 3.0))
GU = 0.5
EPS = 1e-5
f32i = np.float32


class Ct:
    def __init__(self, nc, name):
        self.h = nc.alloc_semaphore(name)
        self.n = 0

    def inc(self, inst, k=1):
        inst.then_inc(self.h, k)
        self.n += k
        return self.n


class Qt:

    def __init__(self, nc, name):
        self.hh = [nc.alloc_semaphore(name + "a"),
                   nc.alloc_semaphore(name + "b")]
        self.nn = [0, 0]
        self.cur = 0

    def chain(self, eng):
        eng.wait_ge(self.hh[self.cur], self.nn[self.cur])

    def inc(self, inst, k=16):
        inst.then_inc(self.hh[self.cur], k)
        self.nn[self.cur] += k
        self.cur ^= 1

    @property
    def snap(self):
        return ("s", self.nn[0], self.nn[1])

    @property
    def mark(self):
        i = self.cur ^ 1
        return ("m", i, self.nn[i])


def qwait(eng, ct, val):
    if isinstance(val, tuple):
        if val[0] == "s":
            if val[1]:
                eng.wait_ge(ct.hh[0], val[1])
            if val[2]:
                eng.wait_ge(ct.hh[1], val[2])
        else:
            eng.wait_ge(ct.hh[val[1]], val[2])
    elif val:
        eng.wait_ge(ct.h, val)


DEBUG_IO = [False]


def build_nc():
    nc = bass.Bass("TRN2", target_bir_lowering=False)

    xf_d = nc.dram_tensor("xf", [NSUB, 128, NCH * 512], F32,
                          kind="ExternalInput").ap()
    wkv_d = nc.dram_tensor("wkv", [32, 128, NCH * 128], F32,
                           kind="ExternalInput").ap()
    bt_d = nc.dram_tensor("bt", [128, 32], F32, kind="ExternalInput").ap()
    rope_d = nc.dram_tensor("rope", [4, 128, 2 * S], F32,
                            kind="ExternalInput").ap()
    wgu_d = nc.dram_tensor("wgu", [NE, NCH, 128, NCH * 256], F32,
                           kind="ExternalInput").ap()
    wd_d = nc.dram_tensor("wd", [NE, 8, 128, NCH * 256], F32,
                          kind="ExternalInput").ap()
    gua_d = nc.dram_tensor("gua", [NL, 128, 2 * NCH * R], F32,
                           kind="ExternalInput").ap()
    gub_d = nc.dram_tensor("gub", [NL, R, 2 * D], F32,
                           kind="ExternalInput").ap()
    da_d = nc.dram_tensor("da", [NL, 128, NCH * R], F32,
                          kind="ExternalInput").ap()
    db_d = nc.dram_tensor("db", [NL, R, D], F32, kind="ExternalInput").ap()
    wr_d = nc.dram_tensor("wr", [128, NCH * 8], F32,
                          kind="ExternalInput").ap()
    temp_d = nc.dram_tensor("temp", [1, 1], F32, kind="ExternalInput").ap()
    out_d = nc.dram_tensor("out", [D, S], F32, kind="ExternalOutput").ap()

    _ik = "ExternalOutput" if DEBUG_IO[0] else "Internal"
    x1_dram = nc.dram_tensor("x1scr", [D, S], F32, kind=_ik).ap()
    z_dram = nc.dram_tensor("zscr", [D, S], BF16, kind=_ik).ap()
    if DEBUG_IO[0]:
        dbgA_d = nc.dram_tensor("dbgA", [128, 16], F32,
                                kind="ExternalOutput").ap()
        dbgM_d = nc.dram_tensor("dbgM", [2, S], F32,
                                kind="ExternalOutput").ap()
        dbgW_d = nc.dram_tensor("dbgW", [8, S], BF16,
                                kind="ExternalOutput").ap()
        dbgL_d = nc.dram_tensor("dbgL", [8, S], F32,
                                kind="ExternalOutput").ap()
        dbgR_d = nc.dram_tensor("dbgR", [128, 96], F32,
                                kind="ExternalOutput").ap()
        dbgY_d = nc.dram_tensor("dbgY", [D, TG], F32,
                                kind="ExternalOutput").ap()
        dbgMid_d = nc.dram_tensor("dbgMid", [D, TG], BF16,
                                  kind="ExternalOutput").ap()

    id8_c = nc.inline_tensor(np.eye(8, dtype=np.float32), "id8c")
    id128_c = nc.inline_tensor(np.eye(128, dtype=np.float32), "id128c")
    import ml_dtypes as _mld
    sel_np = np.zeros((8, 1024), np.float32)
    for e in range(8):
        sel_np[e, e * 128:(e + 1) * 128] = 1.0
    sel_c = nc.inline_tensor(sel_np.astype(_mld.bfloat16), "selc")

    qS = Qt(nc, "qS")
    qG = Qt(nc, "qG")
    qA = Qt(nc, "qA")
    PE = Ct(nc, "cPE")
    VE = Ct(nc, "cVE")
    AC = Ct(nc, "cAC")
    PO = Ct(nc, "cPO")

    _es0 = ExitStack()
    ones_col = _es0.enter_context(nc.sbuf_tensor("ones_col", [128, 1], BF16))
    ones_row = _es0.enter_context(nc.sbuf_tensor("ones_row", [1, 128], F32))
    id8 = _es0.enter_context(nc.sbuf_tensor("id8", [8, 8], F32))
    id128 = _es0.enter_context(nc.sbuf_tensor("id128", [128, 128], F32))
    invT = _es0.enter_context(nc.sbuf_tensor("invT", [1, 1], F32))
    c2pi = _es0.enter_context(nc.sbuf_tensor("c2pi", [128, 1], F32))
    cmag = _es0.enter_context(nc.sbuf_tensor("cmag", [128, 1], F32))
    invT_bc = _es0.enter_context(nc.sbuf_tensor("invT_bc", [128, 1], F32))
    tempc = _es0.enter_context(nc.sbuf_tensor("tempc", [1, 1], F32))
    btb = _es0.enter_context(nc.sbuf_tensor("btb", [128, 32], F32))
    wrb = _es0.enter_context(nc.sbuf_tensor("wrb", [128, NCH * 8], BF16))
    A_acc = _es0.enter_context(nc.sbuf_tensor("A_acc", [128, 16], F32))
    wT = _es0.enter_context(nc.sbuf_tensor("wT", [8, S], BF16))
    psA = _es0.enter_context(nc.psum_tensor("psA", [128, 2048], F32))
    psB = _es0.enter_context(nc.psum_tensor("psB", [128, 2048], F32))
    block = _es0.enter_context(nc.Block())
    T_, V_, A_, P_, S_ = (block.tensor, block.vector, block.scalar,
                          block.gpsimd, block.sync)
    PS = [psA, psB]


    def f(sy):
        for dst, src in ((id8, id8_c), (id128, id128_c)):
            qS.inc(sy.dma_start(out=dst[:, :], in_=src.ap()[:, :]), 16)
        qS.inc(sy.dma_start(out=btb[:, :], in_=bt_d[:, :]), 16)
        qS.inc(sy.dma_start(out=tempc[:, :], in_=temp_d[:, :]), 16)
    S_(f)
    ld0 = qS.snap

    def f(g):
        qG.chain(g)
        qG.inc(g.dma_start(out=wrb[:, :], in_=wr_d[:, :]), 16)
    P_(f)
    wrb_ld = qG.mark

    def f(v):
        v.memset(ones_col[:], 1.0)
        v.memset(ones_row[:], 1.0)
        v.memset(c2pi[:], INV2PI)
        v.memset(cmag[:], MAGIC)
        qwait(v, qS, ld0)
        VE.inc(v.reciprocal(invT[:, :], tempc[:, :]))
    V_(f)
    m_ve = VE.n

    def f(t, m_ve=m_ve):
        t.wait_ge(VE.h, m_ve)
        PE.inc(t.matmul(psA[0:128, 0:1], ones_row[:, :], invT[:, :],
                        start=True, stop=True))
    T_(f)
    m_pe = PE.n

    def f(v, m_pe=m_pe):
        v.wait_ge(PE.h, m_pe)
        VE.inc(v.tensor_copy(invT_bc[:, :], psA[0:128, 0:1]))
    V_(f)
    invT_mark = VE.n

    def ln_stats(conv_src, bO, vrow, xb16, xq16, mrow, rrow, waits,
                 war_convert=None, bank_wait=None, conv_eng="v"):
        xb16_v = xb16[:].rearrange("p (c w) -> p c w", c=NCH)
        xq16_v = xq16[:].rearrange("p (c w) -> p c w", c=NCH)

        CE, Csect = (PO, P_) if conv_eng == "p" else (VE, V_)

        def f(p):
            for ct, val in waits:
                qwait(p, ct, val)
            if war_convert is not None:
                qwait(p, war_convert[0], war_convert[1])
            CE.inc(p.tensor_copy(xb16_v, conv_src))
        Csect(f)
        m_po = CE.n

        def f(a):
            for ct, val in waits:
                qwait(a, ct, val)
            if war_convert is not None:
                qwait(a, war_convert[0], war_convert[1])
            AC.inc(a.activation(xq16_v, conv_src, AF.Square))
        A_(f)
        m_ac = AC.n

        def f(t, m_po=m_po, m_ac=m_ac):
            t.wait_ge(CE.h, m_po)
            if bank_wait is not None:
                t.wait_ge(bank_wait[0].h, bank_wait[1])
            for ch in range(NCH):
                t.matmul(bO[0:1, 0:512], ones_col[:, :],
                         xb16[:, ch * 512:(ch + 1) * 512],
                         start=(ch == 0), stop=(ch == NCH - 1))
            t.wait_ge(AC.h, m_ac)
            for ch in range(NCH):
                inst = t.matmul(bO[0:1, 512:1024], ones_col[:, :],
                                xq16[:, ch * 512:(ch + 1) * 512],
                                start=(ch == 0), stop=(ch == NCH - 1))
            PE.inc(inst)
        T_(f)
        conv_pe = PE.n

        def f(v, conv_pe=conv_pe):
            v.wait_ge(PE.h, conv_pe)
            VE.inc(v.tensor_scalar(mrow, bO[0:1, 0:512], 1.0 / D, None,
                                   ALU.mult))
            v.wait_ge(VE.h, VE.n)
            VE.inc(v.tensor_tensor(vrow, mrow, mrow, ALU.mult))
            v.wait_ge(VE.h, VE.n)
            VE.inc(v.scalar_tensor_tensor(vrow, bO[0:1, 512:1024],
                                          1.0 / D, vrow,
                                          ALU.mult, ALU.subtract))
        V_(f)
        m_fin = VE.n

        def f(a, m_fin=m_fin):
            a.wait_ge(VE.h, m_fin)
            AC.inc(a.activation(vrow, vrow, AF.Ln, bias=f32i(EPS)))
            a.wait_ge(AC.h, AC.n)
            AC.inc(a.activation(rrow, vrow, AF.Exp, scale=f32i(-0.5)))
        A_(f)
        m_r = AC.n

        def f(t, m_r=m_r):
            t.wait_ge(AC.h, m_r)
            t.matmul(bO[:, 0:512], ones_row[:, :], mrow,
                     start=True, stop=True)
            PE.inc(t.matmul(bO[:, 512:1024], ones_row[:, :], rrow,
                            start=True, stop=True))
        T_(f)
        return {"conv_pe": conv_pe, "bc_pe": PE.n,
                "mb": bO[:, 0:512], "rb": bO[:, 512:1024]}

    with (
        nc.sbuf_tensor("m1", [1, S], F32) as m1,
        nc.sbuf_tensor("rstd1", [1, S], F32) as rstd1,
    ):
        with nc.sbuf_tensor("ln1b", [128, NCH * S], BF16) as ln1b:
            with (
                nc.sbuf_tensor("xsubA", [128, 2 * NCH * 512], F32) as xsub,
                nc.sbuf_tensor("xb16A", [128, NCH * 512], BF16) as xb16,
                nc.sbuf_tensor("xq16A", [128, NCH * 512], BF16) as xq16,
                nc.sbuf_tensor("mbcA", [128, 2 * 512], F32) as mbcA,
                nc.sbuf_tensor("rbcA", [128, 2 * 512], F32) as rbcA,
                nc.sbuf_tensor("varA", [1, 1024], F32) as varA,
            ):
                applyA = [None] * NSUB
                statA = [None] * NSUB
                for s in range(NSUB):
                    par = s % 2
                    xs = xsub[:, par * NCH * 512:(par + 1) * NCH * 512]
                    bO = PS[par]

                    def f(sy, s=s, xs=xs):
                        if s >= 2:
                            vm, pm = applyA[s - 2]
                            sy.wait_ge(VE.h, vm)
                            sy.wait_ge(PO.h, pm)
                        qS.chain(sy)
                        qS.inc(sy.dma_start(out=xs[:, :], in_=xf_d[s]), 16)
                    S_(f)
                    dm = qS.mark

                    st = ln_stats(
                        xs.rearrange("p (c w) -> p c w", c=NCH),
                        bO, varA[0:1, par * 512:(par + 1) * 512], xb16, xq16,
                        m1[0:1, s * 512:(s + 1) * 512],
                        rstd1[0:1, s * 512:(s + 1) * 512],
                        waits=[(qS, dm)],
                        war_convert=((PE, statA[s - 1]) if s >= 1
                                     else None))
                    statA[s] = st["conv_pe"]

                    mb = mbcA[:, par * 512:(par + 1) * 512]
                    rb = rbcA[:, par * 512:(par + 1) * 512]

                    def f(a, st=st, mb=mb, rb=rb, s=s):
                        a.wait_ge(PE.h, st["bc_pe"])
                        if s >= 2:
                            vm, pm = applyA[s - 2]
                            a.wait_ge(VE.h, vm)
                            a.wait_ge(PO.h, pm)
                        a.activation(mb, st["mb"], AF.Copy)
                        AC.inc(a.activation(rb, st["rb"], AF.Copy))
                    A_(f)
                    em = AC.n

                    CV = 13
                    ln1b_v = ln1b[:].rearrange("p (c w) -> p c w", c=NCH)
                    xs_v = xs.rearrange("p (c w) -> p c w", c=NCH)

                    def fv(v, s=s, mb=mb, rb=rb, em=em, xs_v=xs_v, dm=dm):
                        qwait(v, qS, dm)
                        v.wait_ge(AC.h, em)
                        xw = xs_v[:, 0:CV, :]
                        v.tensor_tensor(
                            xw, xw,
                            mb.unsqueeze(1).broadcast_to([128, CV, 512]),
                            ALU.subtract)
                        VE.inc(v.tensor_tensor(
                            ln1b_v[:, 0:CV, s * 512:(s + 1) * 512], xw,
                            rb.unsqueeze(1).broadcast_to([128, CV, 512]),
                            ALU.mult))
                    V_(fv)

                    def fp(p, s=s, mb=mb, rb=rb, em=em, xs_v=xs_v, dm=dm):
                        qwait(p, qS, dm)
                        p.wait_ge(AC.h, em)
                        nc_ = NCH - CV
                        xw = xs_v[:, CV:NCH, :]
                        p.tensor_tensor(
                            xw, xw,
                            mb.unsqueeze(1).broadcast_to([128, nc_, 512]),
                            ALU.subtract)
                        PO.inc(p.tensor_tensor(
                            ln1b_v[:, CV:NCH, s * 512:(s + 1) * 512], xw,
                            rb.unsqueeze(1).broadcast_to([128, nc_, 512]),
                            ALU.mult))
                    P_(fp)
                    applyA[s] = (VE.n, PO.n)

            with (
                nc.sbuf_tensor("wsl", [128, 3 * NCH * 128], BF16) as wsl,
                nc.sbuf_tensor("ropeT", [128, 2 * S], F32) as ropeT,
                nc.sbuf_tensor("kv1", [128, S], F32) as kv1,
                nc.sbuf_tensor("kv2", [128, S], F32) as kv2,
                nc.sbuf_tensor("kv3", [128, S], F32) as kv3,
                nc.sbuf_tensor("kv4", [128, S], F32) as kv4,
                nc.sbuf_tensor("rt1", [128, S], F32) as rt1,
                nc.sbuf_tensor("rt2", [128, S], F32) as rt2,
                nc.sbuf_tensor("cB1", [128, S], F32) as cB1,
                nc.sbuf_tensor("cB2", [128, S], F32) as cB2,
                nc.sbuf_tensor("cB3", [128, S], F32) as cB3,
                nc.sbuf_tensor("cB4", [128, S], F32) as cB4,
            ):
                jA = [(VE, applyA[2][0]), (PO, applyA[2][1]),
                      (VE, applyA[3][0]), (PO, applyA[3][1]),
                      (PE, statA[3]), (AC, AC.n)]

                def f(g):
                    for ct, val in jA:
                        qwait(g, ct, val)
                P_(f)

                def f(sy):
                    for ct, val in jA:
                        qwait(sy, ct, val)
                S_(f)
                kv_tiles = {"kv1": kv1, "kv2": kv2, "kv3": kv3, "kv4": kv4}
                tile_free = {k: None for k in kv_tiles}
                slot_free = [None] * 6
                slot_rr = [0]
                bank_free = [[None] * 4 for _ in range(2)]
                chunk_ct = [0]
                firstB = [True]

                def kv_chunk_and_evac(tname, col):
                    slot = slot_rr[0] % 3
                    slot_rr[0] += 1

                    def f(g, slot=slot, col=col):
                        if slot_free[slot] is not None:
                            ct, val = slot_free[slot]
                            qwait(g, ct, val)
                        dst = wsl[:, slot * NCH * 128:
                                  (slot + 1) * NCH * 128]
                        qG.chain(g)
                        qG.inc(g.dma_start(out=dst, in_=wkv_d[col]), 16)
                    P_(f)
                    ld = qG.mark
                    par = chunk_ct[0] % 2
                    chunk_ct[0] += 1
                    bO = PS[par]

                    ts_pe = [0] * 4

                    def f(t, slot=slot, ld=ld, bO=bO, par=par, ts_pe=ts_pe):
                        if firstB[0]:
                            for ct, val in jA:
                                qwait(t, ct, val)
                            firstB[0] = False
                        qwait(t, qG, ld)
                        for ts in range(4):
                            if bank_free[par][ts] is not None:
                                t.wait_ge(AC.h, bank_free[par][ts])
                            for ch in range(NCH):
                                inst = t.matmul(
                                    bO[:, ts * 512:(ts + 1) * 512],
                                    wsl[:, slot * NCH * 128 + ch * 128:
                                        slot * NCH * 128 + (ch + 1) * 128],
                                    ln1b[:, ch * S + ts * 512:
                                         ch * S + (ts + 1) * 512],
                                    start=(ch == 0), stop=(ch == NCH - 1))
                            ts_pe[ts] = PE.inc(inst)
                    T_(f)
                    pm = PE.n
                    slot_free[slot] = (PE, pm)
                    dst = kv_tiles[tname]
                    tf = tile_free[tname]

                    def f(a, bO=bO, dst=dst, col=col, ts_pe=ts_pe, tf=tf,
                          par=par):
                        if tf is not None:
                            a.wait_ge(tf[0].h, tf[1])
                        for ts in range(4):
                            a.wait_ge(PE.h, ts_pe[ts])
                            bank_free[par][ts] = AC.inc(a.activation(
                                dst[:, ts * 512:(ts + 1) * 512],
                                bO[:, ts * 512:(ts + 1) * 512],
                                AF.Identity, bias=btb[:, col:col + 1]))
                    A_(f)
                    return AC.n

                cos_prev_ac = {"v": 0, "p": 0}

                def emit_cos(src, waits, eng="v"):
                    if eng == "v":
                        E, s1_, s2_, sect = VE, cB1, cB2, V_
                    else:
                        E, s1_, s2_, sect = PO, cB3, cB4, P_

                    def f(x, src=src, waits=waits, pv=cos_prev_ac[eng],
                          E=E, s1_=s1_, s2_=s2_, eng=eng):
                        for ct, val in waits:
                            qwait(x, ct, val)
                        if eng == "v":
                            E.inc(x.tensor_scalar(s1_[:, :], src[:, :],
                                                  INV2PI, MAGIC,
                                                  ALU.mult, ALU.add))
                            x.wait_ge(E.h, E.n)
                            E.inc(x.tensor_scalar(s1_[:, :], s1_[:, :],
                                                  MAGIC, None,
                                                  ALU.subtract))
                            x.wait_ge(E.h, E.n)
                            x.wait_ge(AC.h, pv)
                            E.inc(x.scalar_tensor_tensor(
                                s2_[:, :], src[:, :], INV2PI, s1_[:, :],
                                ALU.mult, ALU.subtract))
                        else:
                            x.wait_ge(AC.h, pv)
                            cb = c2pi[:, 0:1].broadcast_to([128, S])
                            mb_ = cmag[:, 0:1].broadcast_to([128, S])
                            x.tensor_tensor(s1_[:, :], src[:, :], cb,
                                            ALU.mult)
                            x.tensor_tensor(s2_[:, :], s1_[:, :], mb_,
                                            ALU.add)
                            x.tensor_tensor(s2_[:, :], s2_[:, :], mb_,
                                            ALU.subtract)
                            E.inc(x.tensor_tensor(s2_[:, :], s1_[:, :],
                                                  s2_[:, :], ALU.subtract))
                    sect(f)
                    vm = E.n

                    def f(a, src=src, vm=vm, E=E, s2_=s2_):
                        a.wait_ge(E.h, vm)
                        AC.inc(a.activation(s2_[:, :], s2_[:, :], AF.Abs))
                        a.wait_ge(AC.h, AC.n)
                        AC.inc(a.activation(src[:, :], s2_[:, :], AF.Sin,
                                            bias=f32i(PIH),
                                            scale=f32i(-TWOPI)))
                    A_(f)
                    cos_prev_ac[eng] = AC.n
                    return AC.n

                rope_mark = [0]
                rope_free = [None]

                def rotary(lo, hi, waits):
                    def f(v, lo=lo, hi=hi, waits=waits):
                        for ct, val in waits:
                            qwait(v, ct, val)
                        qwait(v, qS, rope_mark[0])
                        v.tensor_tensor(rt1[:, :], lo[:, :], ropeT[:, 0:S],
                                        ALU.mult)
                        v.tensor_tensor(rt2[:, :], hi[:, :], ropeT[:, S:2 * S],
                                        ALU.mult)
                        v.tensor_tensor(rt1[:, :], rt1[:, :], rt2[:, :],
                                        ALU.subtract)
                        v.tensor_tensor(rt2[:, :], hi[:, :],
                                        ropeT[:, 0:S], ALU.mult)
                        v.tensor_tensor(cB3[:, :], lo[:, :],
                                        ropeT[:, S:2 * S], ALU.mult)
                        v.tensor_tensor(hi[:, :], rt2[:, :], cB3[:, :],
                                        ALU.add)
                        VE.inc(v.tensor_copy(lo[:, :], rt1[:, :]))
                    V_(f)
                    return VE.n, VE.n

                groups = []
                pj = 8
                for c in range(4):
                    groups.append(("quad", c))
                    groups.append(("plain", pj))
                    groups.append(("plain", pj + 1))
                    pj += 2
                plain_ct = [0]
                for kind, c in groups:
                    if kind == "quad":
                        def f(sy, c=c):
                            if rope_free[0] is not None:
                                sy.wait_ge(rope_free[0][0].h,
                                           rope_free[0][1])
                            qS.chain(sy)
                            qS.inc(sy.dma_start(out=ropeT[:, :],
                                                in_=rope_d[c]), 16)
                        S_(f)
                        rope_mark[0] = qS.mark

                        kv_chunk_and_evac("kv1", c)
                        em2 = kv_chunk_and_evac("kv2", c + 4)
                        kv_chunk_and_evac("kv3", 16 + c)
                        em4 = kv_chunk_and_evac("kv4", 16 + c + 4)

                        vK, pK = rotary(kv1, kv2, [(AC, em2)])
                        c2 = emit_cos(kv2, [(VE, pK)], eng="v")
                        c1 = emit_cos(kv1, [(VE, vK)], eng="v")
                        vV, pV = rotary(kv3, kv4, [(AC, em4), (AC, c1),
                                                   (AC, c2)])
                        c4 = emit_cos(kv4, [(VE, pV)], eng="v")
                        c3 = emit_cos(kv3, [(VE, vV)], eng="v")
                        rope_free[0] = (VE, pV)

                        def f(v, c=c, m=max(c3, c4, c1, c2)):
                            v.wait_ge(AC.h, m)
                            VE.inc(v.tensor_tensor(cB1[:, :], kv1[:, :],
                                                   kv3[:, :], ALU.mult))
                            v.wait_ge(VE.h, VE.n)
                            VE.inc(v.reduce_sum(A_acc[:, c:c + 1], cB1[:, :],
                                                AX.X))
                            VE.inc(v.tensor_tensor(cB2[:, :], kv2[:, :],
                                                   kv4[:, :], ALU.mult))
                            v.wait_ge(VE.h, VE.n)
                            VE.inc(v.reduce_sum(A_acc[:, c + 4:c + 5],
                                                cB2[:, :], AX.X))
                        V_(f)
                        fm = (VE, VE.n)
                        for k in kv_tiles:
                            tile_free[k] = fm
                    else:
                        if plain_ct[0] % 2 == 0:
                            tK, tV = "kv1", "kv3"
                        else:
                            tK, tV = "kv2", "kv4"
                        plain_ct[0] += 1
                        emK = kv_chunk_and_evac(tK, c)
                        emV = kv_chunk_and_evac(tV, 16 + c)
                        cK = emit_cos(kv_tiles[tK], [(AC, emK)], eng="v")
                        cV = emit_cos(kv_tiles[tV], [(AC, emV)], eng="v")

                        def f(v, j=c, m=max(cK, cV), ta=kv_tiles[tK],
                              tb=kv_tiles[tV]):
                            v.wait_ge(AC.h, m)
                            VE.inc(v.tensor_tensor(cB1[:, :], ta[:, :],
                                                   tb[:, :], ALU.mult))
                            v.wait_ge(VE.h, VE.n)
                            VE.inc(v.reduce_sum(A_acc[:, j:j + 1],
                                                cB1[:, :], AX.X))
                        V_(f)
                        fm = (VE, VE.n)
                        tile_free[tK] = fm
                        tile_free[tV] = fm

                def f(v):
                    v.wait_ge(VE.h, VE.n)
                    VE.inc(v.tensor_scalar(A_acc[:, :], A_acc[:, :],
                                           0.5, 0.5, ALU.mult, ALU.add))
                V_(f)
                A_ready = VE.n
                if DEBUG_IO[0]:
                    def f(a, ar=A_ready):
                        a.wait_ge(VE.h, ar)
                        qA.inc(a.dma_start(out=dbgA_d[:, :],
                                           in_=A_acc[:, :]), 16)
                        qA.inc(a.dma_start(out=dbgM_d[0:1, :],
                                           in_=m1[0:1, :]), 16)
                        qA.inc(a.dma_start(out=dbgM_d[1:2, :],
                                           in_=rstd1[0:1, :]), 16)
                    A_(f)
                endB = [(VE, VE.n), (PO, PO.n), (AC, AC.n), (PE, PE.n),
                        (qS, qS.snap), (qG, qG.snap)]

        with (
            nc.sbuf_tensor("xsubC", [128, 2 * NCH * 512], F32) as xsub,
            nc.sbuf_tensor("ln1f", [128, NCH * 512], F32) as ln1f,
            nc.sbuf_tensor("z16", [128, NCH * 512], BF16) as z16,
            nc.sbuf_tensor("xb16C", [128, NCH * 512], BF16) as xb16,
            nc.sbuf_tensor("xq16C", [128, NCH * 512], BF16) as xq16,
            nc.sbuf_tensor("mbcC", [128, 2 * 512], F32) as mbcC,
            nc.sbuf_tensor("rbcC", [128, 2 * 512], F32) as rbcC,
            nc.sbuf_tensor("logits", [8, S], F32) as logits,
            nc.sbuf_tensor("rsm", [128, 1392], F32) as rsm,
            nc.sbuf_tensor("varC", [1, 1024], F32) as varC,
            nc.sbuf_tensor("mC", [1, 1024], F32) as mC,
            nc.sbuf_tensor("rC", [1, 1024], F32) as rC,
        ):
            z_store = [None] * NSUB
            x1_store = [None] * NSUB
            sub_done = [None] * NSUB
            logit_marks = [None] * NSUB
            logit_marks_pe = [None] * NSUB
            stat_war = [None]
            firstC = [True]

            for s in range(NSUB):
                par = s % 2
                xs = xsub[:, par * NCH * 512:(par + 1) * NCH * 512]
                bO = PS[par]
                mb = mbcC[:, par * 512:(par + 1) * 512]
                rb = rbcC[:, par * 512:(par + 1) * 512]

                def f(sy, s=s, xs=xs):
                    if s == 0:
                        for ct, val in endB:
                            qwait(sy, ct, val)
                    if s >= 2:
                        vm, pm = sub_done[s - 2]
                        sy.wait_ge(VE.h, vm)
                        sy.wait_ge(PO.h, pm)
                        qwait(sy, qA, x1_store[s - 2])
                    qS.chain(sy)
                    qS.inc(sy.dma_start(out=xs[:, :], in_=xf_d[s]), 16)
                S_(f)
                dm = qS.mark

                def f(t, s=s, bO=bO):
                    if firstC[0]:
                        for ct, val in endB:
                            qwait(t, ct, val)
                        firstC[0] = False
                    if s >= 2:
                        t.wait_ge(VE.h, logit_marks[s - 2])
                    t.matmul(bO[:, 0:512], ones_row[:, :],
                             m1[0:1, s * 512:(s + 1) * 512],
                             start=True, stop=True)
                    PE.inc(t.matmul(bO[:, 512:1024], ones_row[:, :],
                                    rstd1[0:1, s * 512:(s + 1) * 512],
                                    start=True, stop=True))
                T_(f)
                bc1 = PE.n

                def f(a, bO=bO, bc1=bc1, s=s, mb=mb, rb=rb):
                    a.wait_ge(PE.h, bc1)
                    if s >= 2:
                        vm, pm = sub_done[s - 2]
                        a.wait_ge(VE.h, vm)
                        a.wait_ge(PO.h, pm)
                    a.activation(mb, bO[:, 0:512], AF.Copy)
                    AC.inc(a.activation(rb, bO[:, 512:1024], AF.Copy))
                A_(f)
                em1 = AC.n

                CV = 13
                xs_v = xs.rearrange("p (c w) -> p c w", c=NCH)
                ln1f_v = ln1f[:].rearrange("p (c w) -> p c w", c=NCH)
                z16_v = z16[:].rearrange("p (c w) -> p c w", c=NCH)

                def mbc3(bc, n):
                    return bc.unsqueeze(1).broadcast_to([128, n, 512])

                def fv(v, xs_v=xs_v, dm=dm, em1=em1, mb=mb, rb=rb):
                    qwait(v, qS, dm)
                    v.wait_ge(AC.h, em1)
                    v.wait_ge(VE.h, A_ready)
                    d = ln1f_v[:, 0:CV, :]
                    v.tensor_tensor(d, xs_v[:, 0:CV, :], mbc3(mb, CV),
                                    ALU.subtract)
                    v.tensor_tensor(d, d, mbc3(rb, CV), ALU.mult)
                    VE.inc(v.tensor_tensor(
                        d, d, A_acc[:, 0:CV].unsqueeze(2).broadcast_to(
                            [128, CV, 512]), ALU.mult))
                V_(fv)

                def fp(p, xs_v=xs_v, dm=dm, em1=em1, mb=mb, rb=rb):
                    qwait(p, qS, dm)
                    p.wait_ge(AC.h, em1)
                    p.wait_ge(VE.h, A_ready)
                    nc_ = NCH - CV
                    d = ln1f_v[:, CV:NCH, :]
                    p.tensor_tensor(d, xs_v[:, CV:NCH, :], mbc3(mb, nc_),
                                    ALU.subtract)
                    p.tensor_tensor(d, d, mbc3(rb, nc_), ALU.mult)
                    PO.inc(p.tensor_tensor(
                        d, d, A_acc[:, CV:NCH].unsqueeze(2).broadcast_to(
                            [128, nc_, 512]), ALU.mult))
                P_(fp)
                t3_marks = [(VE, VE.n), (PO, PO.n)]

                st2 = ln_stats(
                    ln1f_v,
                    bO, varC[0:1, par * 512:(par + 1) * 512], xb16, xq16,
                    mC[0:1, par * 512:(par + 1) * 512],
                    rC[0:1, par * 512:(par + 1) * 512],
                    waits=t3_marks,
                    war_convert=((PE, stat_war[0]) if stat_war[0]
                                 is not None else None))
                stat_war[0] = st2["conv_pe"]

                def f(a, st2=st2, mb=mb, rb=rb):
                    a.wait_ge(PE.h, st2["bc_pe"])
                    a.activation(mb, st2["mb"], AF.Copy)
                    AC.inc(a.activation(rb, st2["rb"], AF.Copy))
                A_(f)
                em2 = AC.n

                def fv(v, xs_v=xs_v, em2=em2, mb=mb, rb=rb):
                    v.wait_ge(AC.h, em2)
                    d = ln1f_v[:, 0:CV, :]
                    v.tensor_tensor(d, d, mbc3(mb, CV), ALU.subtract)
                    v.tensor_tensor(d, d, mbc3(rb, CV), ALU.mult)
                    VE.inc(v.tensor_tensor(xs_v[:, 0:CV, :],
                                           xs_v[:, 0:CV, :], d, ALU.add))
                V_(fv)

                def fp(p, xs_v=xs_v, em2=em2, mb=mb, rb=rb):
                    p.wait_ge(AC.h, em2)
                    nc_ = NCH - CV
                    d = ln1f_v[:, CV:NCH, :]
                    p.tensor_tensor(d, d, mbc3(mb, nc_), ALU.subtract)
                    p.tensor_tensor(d, d, mbc3(rb, nc_), ALU.mult)
                    PO.inc(p.tensor_tensor(xs_v[:, CV:NCH, :],
                                           xs_v[:, CV:NCH, :], d, ALU.add))
                P_(fp)
                x1_marks = [(VE, VE.n), (PO, PO.n)]

                def f(a, s=s, xs=xs, x1m=x1_marks):
                    for ct, val in x1m:
                        qwait(a, ct, val)
                    qA.chain(a)
                    qA.inc(a.dma_start(
                        out=x1_dram[:, s * 512:(s + 1) * 512].rearrange(
                            "(c p) w -> p c w", p=128),
                        in_=xs.rearrange("p (c w) -> p c w", c=NCH)), 16)
                A_(f)
                x1_store[s] = qA.mark

                st3 = ln_stats(
                    xs_v,
                    bO, varC[0:1, par * 512:(par + 1) * 512], xb16, xq16,
                    mC[0:1, par * 512:(par + 1) * 512],
                    rC[0:1, par * 512:(par + 1) * 512],
                    waits=x1_marks, war_convert=(PE, stat_war[0]))
                stat_war[0] = st3["conv_pe"]

                def f(a, st3=st3, mb=mb, rb=rb):
                    a.wait_ge(PE.h, st3["bc_pe"])
                    a.activation(mb, st3["mb"], AF.Copy)
                    AC.inc(a.activation(rb, st3["rb"], AF.Copy))
                A_(f)
                em3 = AC.n

                def fv(v, xs_v=xs_v, em3=em3, s=s, mb=mb, rb=rb):
                    v.wait_ge(AC.h, em3)
                    if s >= 1:
                        qwait(v, qA, z_store[s - 1])
                    d = ln1f_v[:, 0:CV, :]
                    v.tensor_tensor(d, xs_v[:, 0:CV, :], mbc3(mb, CV),
                                    ALU.subtract)
                    VE.inc(v.tensor_tensor(z16_v[:, 0:CV, :], d,
                                           mbc3(rb, CV), ALU.mult))
                V_(fv)

                def fp(p, xs_v=xs_v, em3=em3, s=s, mb=mb, rb=rb):
                    p.wait_ge(AC.h, em3)
                    if s >= 1:
                        qwait(p, qA, z_store[s - 1])
                    nc_ = NCH - CV
                    d = ln1f_v[:, CV:NCH, :]
                    p.tensor_tensor(d, xs_v[:, CV:NCH, :], mbc3(mb, nc_),
                                    ALU.subtract)
                    PO.inc(p.tensor_tensor(z16_v[:, CV:NCH, :], d,
                                           mbc3(rb, nc_), ALU.mult))
                P_(fp)
                sub_done[s] = (VE.n, PO.n)
                zm = sub_done[s]

                def f(a, s=s, zm=zm):
                    a.wait_ge(VE.h, zm[0])
                    a.wait_ge(PO.h, zm[1])
                    qA.chain(a)
                    qA.inc(a.dma_start(
                        out=z_dram[:, s * 512:(s + 1) * 512].rearrange(
                            "(c p) w -> p c w", p=128),
                        in_=z16[:].rearrange("p (c w) -> p c w", c=NCH)),
                        16)
                A_(f)
                z_store[s] = qA.mark

                def f(t, s=s, bO=bO, zm=zm, em3=em3):
                    t.wait_ge(VE.h, zm[0])
                    t.wait_ge(PO.h, zm[1])
                    t.wait_ge(AC.h, em3)
                    qwait(t, qG, wrb_ld)
                    for ch in range(NCH):
                        inst = t.matmul(
                            bO[0:8, 0:512], wrb[:, ch * 8:(ch + 1) * 8],
                            z16[:, ch * 512:(ch + 1) * 512],
                            start=(ch == 0), stop=(ch == NCH - 1))
                    PE.inc(inst)
                T_(f)
                lg = PE.n
                logit_marks_pe[s] = PE.n

                def f(v, s=s, bO=bO, lg=lg):
                    v.wait_ge(PE.h, lg)
                    VE.inc(v.tensor_copy(
                        logits[0:8, s * 512:(s + 1) * 512],
                        bO[0:8, 0:512]))
                V_(f)
                logit_marks[s] = VE.n

            col = {}
            for i, nm in enumerate(["lgT", "inh", "dist", "iz", "w",
                                    "fb", "mask", "fbw", "wf", "t1"]):
                col[nm] = rsm[:, i * 128:(i + 1) * 128]
            sc = {}
            for i, nm in enumerate(["wsum", "noex", "m1r", "cnt1", "m2r",
                                    "fs", "ws"]):
                sc[nm] = rsm[:, 1280 + i * 16:1280 + (i + 1) * 16]

            def c3(nm):
                return col[nm].rearrange("p (c e) -> p c e", c=NCH)

            def s3(nm):
                return sc[nm].unsqueeze(2).broadcast_to([128, NCH, 8])

            tp_marks = [None, None]
            cp_marks = [None, None]
            for k in range(NCH):
                kp = k % 2
                base = 1536 if kp == 0 else 1024
                ps_t = psB[0:128, base:base + 8]

                def f(t, k=k, ps_t=ps_t, w=(cp_marks[k % 2]
                                            if k >= 2 else None)):
                    if k == 0:
                        t.wait_ge(VE.h, logit_marks[NSUB - 1])
                    if w is not None:
                        t.wait_ge(VE.h, w)
                    PE.inc(t.transpose(
                        ps_t, logits[0:8, k * 128:(k + 1) * 128],
                        id8[:, :]))
                T_(f)
                tp_marks[kp] = PE.n

                def f(v, k=k, ps_t=ps_t, tp=PE.n):
                    v.wait_ge(PE.h, tp)
                    VE.inc(v.tensor_copy(col["lgT"][:, k * 8:(k + 1) * 8],
                                         ps_t))
                V_(f)
                cp_marks[kp] = VE.n
            gather_done = VE.n

            def f(a, gd=gather_done):
                a.wait_ge(VE.h, gd)
                a.wait_ge(VE.h, invT_mark)
                AC.inc(a.activation(col["inh"], col["lgT"], AF.Sigmoid,
                                    scale=invT_bc[:, 0:1]))
                a.wait_ge(AC.h, AC.n)
                AC.inc(a.activation(col["dist"], col["inh"], AF.Abs,
                                    bias=f32i(-GC)))
                a.wait_ge(AC.h, AC.n)
                AC.inc(a.activation(col["w"], col["dist"], AF.Exp,
                                    scale=f32i(-10.0)))
                AC.inc(a.activation(col["fb"], col["dist"], AF.Exp,
                                    scale=f32i(-1.0 / 0.3)))
            A_(f)
            am = AC.n

            def f(v, am=am):
                v.wait_ge(AC.h, am)
                steps = [
                    lambda: v.tensor_scalar(col["iz"], col["inh"], GL,
                                            None, ALU.is_ge),
                    lambda: v.tensor_scalar(col["t1"], col["inh"], GU,
                                            None, ALU.is_le),
                    lambda: v.tensor_tensor(col["iz"], col["iz"],
                                            col["t1"], ALU.mult),
                    lambda: v.tensor_tensor(col["w"], col["w"], col["iz"],
                                            ALU.mult),
                    lambda: v.reduce_sum(sc["wsum"], c3("w"), AX.X),
                    lambda: v.reduce_max(sc["m1r"], c3("fb"), AX.X),
                    lambda: v.tensor_scalar(sc["noex"], sc["wsum"], 1e-8,
                                            None, ALU.is_lt),
                    lambda: v.tensor_tensor(c3("mask"), c3("fb"),
                                            s3("m1r"), ALU.is_ge),
                    lambda: v.reduce_sum(sc["cnt1"], c3("mask"), AX.X),
                    lambda: v.scalar_tensor_tensor(col["t1"], col["mask"],
                                                   -1e30, col["fb"],
                                                   ALU.mult, ALU.add),
                    lambda: v.reduce_max(sc["m2r"], c3("t1"), AX.X),
                    lambda: v.tensor_tensor(c3("t1"), c3("t1"),
                                            s3("m2r"), ALU.is_ge),
                    lambda: v.tensor_scalar(sc["cnt1"], sc["cnt1"], 1.5,
                                            None, ALU.is_lt),
                    lambda: v.tensor_tensor(c3("t1"), c3("t1"),
                                            s3("cnt1"), ALU.mult),
                    lambda: v.tensor_tensor(col["mask"], col["mask"],
                                            col["t1"], ALU.add),
                    lambda: v.tensor_tensor(col["fbw"], col["fb"],
                                            col["mask"], ALU.mult),
                    lambda: v.reduce_sum(sc["fs"], c3("fbw"), AX.X),
                    lambda: v.tensor_scalar(sc["fs"], sc["fs"], 1e-8,
                                            None, ALU.max),
                    lambda: v.reciprocal(sc["fs"], sc["fs"]),
                    lambda: v.tensor_tensor(c3("fbw"), c3("fbw"),
                                            s3("fs"), ALU.mult),
                    lambda: v.tensor_tensor(c3("wf"), c3("fbw"),
                                            s3("noex"), ALU.mult),
                    lambda: v.tensor_tensor(c3("t1"), c3("w"),
                                            s3("noex"), ALU.mult),
                    lambda: v.tensor_tensor(col["t1"], col["t1"],
                                            col["w"], ALU.subtract),
                    lambda: v.tensor_tensor(col["wf"], col["wf"],
                                            col["t1"], ALU.subtract),
                    lambda: v.reduce_sum(sc["ws"], c3("wf"), AX.X),
                    lambda: v.tensor_scalar(sc["ws"], sc["ws"], 1e-8,
                                            None, ALU.max),
                    lambda: v.reciprocal(sc["ws"], sc["ws"]),
                    lambda: v.tensor_tensor(c3("wf"), c3("wf"),
                                            s3("ws"), ALU.mult),
                ]
                for emit in steps:
                    v.wait_ge(VE.h, VE.n)
                    VE.inc(emit())
            V_(f)
            math_done = VE.n

            bk_marks = [None, None]
            for k in range(NCH):
                kp = k % 2
                base = 1024 if kp == 0 else 1536
                ps_t = psB[0:8, base:base + 128]

                def f(t, k=k, ps_t=ps_t, md=math_done,
                      w=(bk_marks[k % 2] if k >= 2 else None)):
                    t.wait_ge(VE.h, md)
                    if w is not None:
                        t.wait_ge(VE.h, w)
                    PE.inc(t.transpose(ps_t,
                                       col["wf"][:, k * 8:(k + 1) * 8],
                                       id128[:, :]))
                T_(f)

                def f(v, k=k, ps_t=ps_t, tb=PE.n):
                    v.wait_ge(PE.h, tb)
                    VE.inc(v.tensor_copy(wT[0:8, k * 128:(k + 1) * 128],
                                         ps_t))
                V_(f)
                bk_marks[kp] = VE.n
            wT_ready = VE.n
            endC = [(VE, VE.n), (PO, PO.n), (AC, AC.n), (PE, PE.n),
                    (qA, qA.snap), (qS, qS.snap), (qG, qG.snap)]
            endC_core = [(VE, sub_done[NSUB - 1][0]),
                         (PO, sub_done[NSUB - 1][1]),
                         (PE, logit_marks_pe[NSUB - 1]),
                         (qA, z_store[NSUB - 1])]

    with (
        nc.sbuf_tensor("zg", [128, NCH * TG], BF16) as zg,
        nc.sbuf_tensor("yacc", [128, NCH * TG], F32) as yacc,
        nc.sbuf_tensor("midD", [128, NCH * TG], BF16) as mid,
        nc.sbuf_tensor("ringD", [128, 2 * NCH * 512], BF16) as ring,
        nc.sbuf_tensor("sgD", [128, TG], F32) as sg,
        nc.sbuf_tensor("wbcD", [128, TG], F32) as wbc,
        nc.sbuf_tensor("yscrD", [128, TG], F32) as yscr,
        nc.sbuf_tensor("hgD", [64, TG], BF16) as hg,
        nc.sbuf_tensor("huD", [64, TG], BF16) as hu,
        nc.sbuf_tensor("hdD", [64, TG], BF16) as hd,
        nc.sbuf_tensor("guaD", [128, 2 * NCH * R], BF16) as gua,
        nc.sbuf_tensor("gubD", [64, 2 * D], BF16) as gub,
        nc.sbuf_tensor("selD", [8, 1024], BF16) as sel,
    ):
        scrV = sg[:, 0:512]
        scrP = sg[:, 512:1024]
        mid_f32 = mid[:].bitcast(F32)
        hg_f = hg[:].bitcast(F32)
        hu_f = hu[:].bitcast(F32)
        hd_f = hd[:].bitcast(F32)
        gua_f = gua[:].bitcast(F32)
        gub_f = gub[:].bitcast(F32)
        epi_rows = {(0, "v"): hg_f[0:1, 0:512], (1, "v"): hu_f[0:1, 0:512],
                    (0, "m"): hd_f[0:1, 0:512], (1, "m"): gua_f[0:1, 0:512],
                    (0, "r"): gua_f[0:1, 512:1024],
                    (1, "r"): gub_f[0:1, 0:512]}
        zb16s = zg[:, 0:NCH * 512]
        zq16s = zg[:, NCH * 512:NCH * 1024]

        firstD = [True]
        prev_zg_free = [None]
        ring_free = [None, None]
        freeA = {0: None, 1: None}
        freeB = {0: None, 1: None}
        lr_free = [None]
        prev_x2_store = [None]
        prev_epi = [None]

        def gwait(g, cond):
            if cond is not None:
                qwait(g, cond[0], cond[1])

        def f(g):
            for ct, val in endC_core:
                qwait(g, ct, val)
        P_(f)
        sel_ld = [None]

        for g_i in range(NG):
            def f(sy, g_i=g_i):
                if g_i == 0:
                    for ct, val in endC_core:
                        qwait(sy, ct, val)
                else:
                    sy.wait_ge(PE.h, prev_zg_free[0])
                qS.chain(sy)
                qS.inc(sy.dma_start(
                    out=zg[:].rearrange("p (c w) -> p c w", c=NCH),
                    in_=z_dram[:, g_i * TG:(g_i + 1) * TG].rearrange(
                        "(c p) w -> p c w", p=128)), 16)
                if g_i == 0:
                    for ct, val in endC:
                        qwait(sy, ct, val)
                    qS.chain(sy)
                    qS.inc(sy.dma_start(out=sel[:, :],
                                        in_=sel_c.ap()[:, :]), 16)
                    sel_ld[0] = qS.mark
            S_(f)
            zl = qS.mark

            sg_free = [None]
            mid15 = [None]
            last_up_pe = [None]
            last_gate_pe = [None]

            for ei, e in enumerate([2, 3, 0, 1, 4, 5, 6, 7]):
                low = e < NL

                h_ac = None
                if low:
                    def f(g, e=e, lf=lr_free[0], g_i=g_i,
                          pe_epi=(prev_epi[0][3] if prev_epi[0] is not None
                                  else None)):
                        gwait(g, lf)
                        if e == 0 and g_i == 0:
                            g.wait_ge(VE.h, wT_ready)
                        if e == 0 and g_i == 1 and pe_epi is not None:
                            g.wait_ge(PE.h, pe_epi)
                        qG.chain(g)
                        qG.inc(g.dma_start(out=gua[:, :],
                                           in_=gua_d[e]), 16)
                        qG.chain(g)
                        qG.inc(g.dma_start(out=gub[:, :],
                                           in_=gub_d[e]), 16)
                    P_(f)
                    lr_ld = qG.snap

                    def f(t, zl=zl, lr=lr_ld, fb0=freeB[0],
                          fb1=freeB[1]):
                        qwait(t, qS, zl)
                        qwait(t, qG, lr)
                        gwait(t, fb0)
                        for ch in range(NCH):
                            for ts in range(2):
                                inst = t.matmul(
                                    psB[0:64, ts * 512:(ts + 1) * 512],
                                    gua[:, ch * R:(ch + 1) * R],
                                    zg[:, ch * TG + ts * 512:
                                       ch * TG + (ts + 1) * 512],
                                    start=(ch == 0),
                                    stop=(ch == NCH - 1))
                        PE.inc(inst)
                        gwait(t, fb1)
                        for ch in range(NCH):
                            for ts in range(2):
                                inst = t.matmul(
                                    psB[0:64, 1024 + ts * 512:
                                        1024 + (ts + 1) * 512],
                                    gua[:, NCH * R + ch * R:
                                        NCH * R + (ch + 1) * R],
                                    zg[:, ch * TG + ts * 512:
                                       ch * TG + (ts + 1) * 512],
                                    start=(ch == 0),
                                    stop=(ch == NCH - 1))
                        PE.inc(inst)
                    T_(f)
                    h_pe = PE.n

                    def f(a, h_pe=h_pe):
                        a.wait_ge(PE.h, h_pe)
                        a.activation(hg[:, :], psB[0:64, 0:1024], AF.Copy)
                        AC.inc(a.activation(hu[:, :],
                                            psB[0:64, 1024:2048],
                                            AF.Copy))
                    A_(f)
                    h_ac = AC.n
                    freeB[0] = (AC, h_ac)
                    freeB[1] = (AC, h_ac)

                for fch in range(NCH):
                    slot = fch % 2
                    sbase = slot * NCH * 256

                    def f(g, sbase=sbase, e=e, fch=fch,
                          rf=ring_free[slot]):
                        gwait(g, rf)
                        qG.chain(g)
                        qG.inc(g.dma_start(
                            out=ring[:, sbase:sbase + NCH * 256],
                            in_=wgu_d[e, fch]), 16)
                    P_(f)
                    ld = qG.mark

                    par = fch % 2
                    gb_ps = psA[:, par * 1024:(par + 1) * 1024]
                    ub_ps = psB[:, par * 1024:(par + 1) * 1024]

                    def fg(t, ld=ld, zl=zl, sbase=sbase, fch=fch,
                           gb_ps=gb_ps, low=low, h_ac=h_ac,
                           fa=freeA[par]):
                        qwait(t, qG, ld)
                        qwait(t, qS, zl)
                        gwait(t, fa)
                        if low:
                            t.wait_ge(AC.h, h_ac)
                        inst = None
                        for ch in range(NCH):
                            for ts in range(2):
                                inst = t.matmul(
                                    gb_ps[:, ts * 512:(ts + 1) * 512],
                                    ring[:, sbase + ch * 128:
                                         sbase + (ch + 1) * 128],
                                    zg[:, ch * TG + ts * 512:
                                       ch * TG + (ts + 1) * 512],
                                    start=(ch == 0),
                                    stop=(ch == NCH - 1 and not low))
                        if low:
                            for ts in range(2):
                                inst = t.matmul(
                                    gb_ps[:, ts * 512:(ts + 1) * 512],
                                    gub[0:64, fch * 128:(fch + 1) * 128],
                                    hg[0:64, ts * 512:(ts + 1) * 512],
                                    start=False, stop=True)
                        PE.inc(inst)
                    T_(fg)
                    gate_pe = PE.n
                    last_gate_pe[0] = gate_pe

                    def fu(t, ld=ld, sbase=sbase, fch=fch, ub_ps=ub_ps,
                           low=low, fb=freeB[par]):
                        gwait(t, fb)
                        inst = None
                        for ch in range(NCH):
                            for ts in range(2):
                                inst = t.matmul(
                                    ub_ps[:, ts * 512:(ts + 1) * 512],
                                    ring[:, sbase + NCH * 128 + ch * 128:
                                         sbase + NCH * 128 +
                                         (ch + 1) * 128],
                                    zg[:, ch * TG + ts * 512:
                                       ch * TG + (ts + 1) * 512],
                                    start=(ch == 0),
                                    stop=(ch == NCH - 1 and not low))
                        if low:
                            for ts in range(2):
                                inst = t.matmul(
                                    ub_ps[:, ts * 512:(ts + 1) * 512],
                                    gub[0:64, D + fch * 128:
                                         D + (fch + 1) * 128],
                                    hu[0:64, ts * 512:(ts + 1) * 512],
                                    start=False, stop=True)
                        PE.inc(inst)
                    T_(fu)
                    up_pe = PE.n
                    last_up_pe[0] = up_pe
                    ring_free[slot] = (PE, up_pe)

                    def fs(a, gate_pe=gate_pe, gb_ps=gb_ps,
                           sgf=sg_free[0]):
                        a.wait_ge(PE.h, gate_pe)
                        if sgf is not None:
                            a.wait_ge(VE.h, sgf)
                        AC.inc(a.activation(sg[:, :], gb_ps, AF.Silu))
                    A_(fs)
                    silu_ac = AC.n
                    freeA[par] = (AC, silu_ac)

                    def fm(v, silu_ac=silu_ac, up_pe=up_pe, ub_ps=ub_ps,
                           fch=fch, g_i=g_i, ei=ei):
                        v.wait_ge(AC.h, silu_ac)
                        v.wait_ge(PE.h, up_pe)
                        if ei == 0 and fch == 0 and g_i == 1:
                            qwait(v, qA, prev_x2_store[0])
                        VE.inc(v.tensor_tensor(
                            mid[:, fch * TG:(fch + 1) * TG],
                            sg[:, :], ub_ps, ALU.mult))
                    V_(fm)
                    mid_ve = VE.n
                    freeB[par] = (VE, mid_ve)
                    sg_free[0] = mid_ve
                    if fch == NCH - 1:
                        mid15[0] = mid_ve

                hd_ac = None
                if low:
                    def f(g, e=e):
                        g.wait_ge(PE.h, last_gate_pe[0])
                        qG.chain(g)
                        qG.inc(g.dma_start(out=gua[:, 0:NCH * R],
                                           in_=da_d[e]), 16)
                        qG.chain(g)
                        qG.inc(g.dma_start(out=gub[:, 0:D],
                                           in_=db_d[e]), 16)
                    P_(f)
                    dadb_ld = qG.snap

                    def f(t, dadb=dadb_ld, m15=mid15[0], fb=freeB[0]):
                        qwait(t, qG, dadb)
                        t.wait_ge(VE.h, m15)
                        gwait(t, fb)
                        inst = None
                        for ch in range(NCH):
                            for ts in range(2):
                                inst = t.matmul(
                                    psB[0:64, ts * 512:(ts + 1) * 512],
                                    gua[:, ch * R:(ch + 1) * R],
                                    mid[:, ch * TG + ts * 512:
                                        ch * TG + (ts + 1) * 512],
                                    start=(ch == 0),
                                    stop=(ch == NCH - 1))
                        PE.inc(inst)
                    T_(f)
                    hd_pe = PE.n

                    def f(a, hd_pe=hd_pe):
                        a.wait_ge(PE.h, hd_pe)
                        AC.inc(a.activation(hd[:, :], psB[0:64, 0:1024],
                                            AF.Copy))
                    A_(f)
                    hd_ac = AC.n
                    freeB[0] = (AC, hd_ac)
                    lr_free[0] = (PE, hd_pe)

                def f(t, e=e, g_i=g_i, fb1=freeB[1]):
                    qwait(t, qS, sel_ld[0])
                    t.wait_ge(VE.h, wT_ready)
                    gwait(t, fb1)
                    inst = None
                    for ts in range(2):
                        inst = t.matmul(
                            psB[:, 1024 + ts * 512:1024 + (ts + 1) * 512],
                            sel[:, e * 128:(e + 1) * 128],
                            wT[0:8, g_i * TG + ts * 512:
                               g_i * TG + (ts + 1) * 512],
                            start=True, stop=True)
                    PE.inc(inst)
                T_(f)
                wbc_pe = PE.n

                def f(v, wbc_pe=wbc_pe):
                    v.wait_ge(PE.h, wbc_pe)
                    VE.inc(v.tensor_copy(wbc[:, :], psB[:, 1024:2048]))
                V_(f)
                wbc_ve = VE.n
                freeB[1] = (VE, wbc_ve)

                for dq in range(8):
                    slot = dq % 2
                    sbase = slot * NCH * 256

                    def f(g, sbase=sbase, e=e, dq=dq,
                          rf=ring_free[slot]):
                        gwait(g, rf)
                        qG.chain(g)
                        qG.inc(g.dma_start(
                            out=ring[:, sbase:sbase + NCH * 256],
                            in_=wd_d[e, dq]), 16)
                    P_(f)
                    ld = qG.mark

                    for dl in range(2):
                        dch = dq * 2 + dl
                        par = dch % 2
                        d_ps = psA[:, par * 1024:(par + 1) * 1024]

                        def f(t, ld=ld, dch=dch, dl=dl, sbase=sbase,
                              d_ps=d_ps, low=low, hd_ac=hd_ac,
                              m15=mid15[0], fa=freeA[par]):
                            qwait(t, qG, ld)
                            t.wait_ge(VE.h, m15)
                            gwait(t, fa)
                            if low:
                                t.wait_ge(AC.h, hd_ac)
                            inst = None
                            for ch in range(NCH):
                                for ts in range(2):
                                    inst = t.matmul(
                                        d_ps[:, ts * 512:(ts + 1) * 512],
                                        ring[:, sbase + ch * 256 +
                                             dl * 128:sbase + ch * 256 +
                                             dl * 128 + 128],
                                        mid[:, ch * TG + ts * 512:
                                            ch * TG + (ts + 1) * 512],
                                        start=(ch == 0),
                                        stop=(ch == NCH - 1 and not low))
                            if low:
                                for ts in range(2):
                                    inst = t.matmul(
                                        d_ps[:, ts * 512:(ts + 1) * 512],
                                        gub[0:64, dch * 128:
                                            (dch + 1) * 128],
                                        hd[0:64, ts * 512:(ts + 1) * 512],
                                        start=False, stop=True)
                            PE.inc(inst)
                        T_(f)
                        down_pe = PE.n
                        if dl == 1:
                            ring_free[slot] = (PE, down_pe)
                        if low and dq == 7 and dl == 1:
                            lr_free[0] = (PE, down_pe)

                        def f(v, down_pe=down_pe, dch=dch, d_ps=d_ps,
                              ei=ei, g_i=g_i):
                            v.wait_ge(PE.h, down_pe)
                            if ei == 0 and dch == 0 and prev_epi[0] \
                                    is not None:
                                v.wait_ge(AC.h, prev_epi[0][0])
                                v.wait_ge(VE.h, prev_epi[0][1])
                            dst = yacc[:, dch * TG:(dch + 1) * TG]
                            if ei == 0:
                                inst = v.tensor_tensor(dst, d_ps, wbc[:, :],
                                                       ALU.mult)
                            else:
                                v.tensor_tensor(yscr[:, :], d_ps, wbc[:, :],
                                                ALU.mult)
                                inst = v.tensor_tensor(dst, dst, yscr[:, :],
                                                       ALU.add)
                            VE.inc(inst)
                        V_(f)
                        freeA[par] = (VE, VE.n)

            prev_zg_free[0] = last_up_pe[0]

            if DEBUG_IO[0] and g_i == 0:
                def f(a):
                    a.wait_ge(VE.h, freeA[0][1])
                    a.wait_ge(VE.h, freeA[1][1])
                    a.wait_ge(VE.h, wT_ready)
                    qA.inc(a.dma_start(out=dbgW_d[:, :], in_=wT[:, :]), 16)
                    qA.chain(a)
                    qA.inc(a.dma_start(
                        out=dbgY_d.rearrange("(c p) w -> p c w", p=128),
                        in_=yacc[:].rearrange("p (c w) -> p c w", c=NCH)), 16)
                    qA.chain(a)
                    qA.inc(a.dma_start(
                        out=dbgMid_d.rearrange("(c p) w -> p c w", p=128),
                        in_=mid[:].rearrange("p (c w) -> p c w", c=NCH)), 16)
                A_(f)

            epi_ac = epi_po = epi_pe = epi_bc = None
            for sub in range(2):
                par = sub % 2
                bO = PS[par]
                scol = g_i * TG + sub * 512

                def f(sy, sub=sub, scol=scol, g_i=g_i):
                    sy.wait_ge(PE.h, ring_free[0][1])
                    sy.wait_ge(PE.h, ring_free[1][1])
                    if sub == 1 or g_i == 1:
                        qwait(sy, qA, prev_x2_store[0])
                    qS.chain(sy)
                    qS.inc(sy.dma_start(
                        out=mid_f32.rearrange("p (c w) -> p c w", c=NCH),
                        in_=x1_dram[:, scol:scol + 512].rearrange(
                            "(c p) w -> p c w", p=128)), 16)
                S_(f)
                x1l = qS.mark

                yw = [(VE, freeA[0][1]), (VE, freeA[1][1])]
                y_v = yacc[:].rearrange("p (c w) -> p c w",
                                        c=NCH)[:, :, sub * 512:
                                               (sub + 1) * 512]
                st4 = ln_stats(
                    y_v,
                    bO, epi_rows[(par, "v")], zb16s, zq16s,
                    epi_rows[(par, "m")], epi_rows[(par, "r")],
                    waits=yw + [(PE, last_up_pe[0])],
                    war_convert=((AC, prev_epi[0][0]) if prev_epi[0]
                                 is not None and sub == 0 else None),
                    bank_wait=(freeA[0] if par == 0 else freeB[0]))
                epi_pe = st4["conv_pe"]
                epi_bc = st4["bc_pe"]

                def f(a, st4=st4):
                    a.wait_ge(PE.h, st4["bc_pe"])
                    a.activation(scrV[:, :], st4["mb"], AF.Copy)
                    AC.inc(a.activation(scrP[:, :], st4["rb"], AF.Copy))
                A_(f)
                em4 = AC.n
                CV = NCH
                mid_v = mid_f32.rearrange("p (c w) -> p c w", c=NCH)

                def mbc3(bc, n):
                    return bc.unsqueeze(1).broadcast_to([128, n, 512])

                def fv(v, y_v=y_v, em4=em4, x1l=x1l):
                    v.wait_ge(AC.h, em4)
                    qwait(v, qS, x1l)
                    d = y_v[:, 0:CV, :]
                    v.tensor_tensor(d, d, mbc3(scrV[:, :], CV),
                                    ALU.subtract)
                    v.tensor_tensor(d, d, mbc3(scrP[:, :], CV), ALU.mult)
                    VE.inc(v.tensor_tensor(mid_v[:, 0:CV, :],
                                           mid_v[:, 0:CV, :], d, ALU.add))
                V_(fv)
                vmark = VE.n

                if par == 0:
                    freeA[0] = (VE, vmark)
                else:
                    freeB[0] = (VE, vmark)
                epi_po = vmark
                epi_ac = AC.n

                def f(a, scol=scol, vm=vmark):
                    a.wait_ge(VE.h, vm)
                    qA.chain(a)
                    qA.inc(a.dma_start(
                        out=out_d[:, scol:scol + 512].rearrange(
                            "(c p) w -> p c w", p=128),
                        in_=mid_f32.rearrange("p (c w) -> p c w",
                                              c=NCH)), 16)
                A_(f)
                prev_x2_store[0] = qA.mark
            prev_epi[0] = (epi_ac, epi_po, epi_pe, epi_bc)

        def f(sy):
            qwait(sy, qA, qA.snap)
        S_(f)

    _es0.close()
    return nc


_ORDER = np.concatenate([np.arange(0, 1024, 2), np.arange(1, 1024, 2),
                         np.arange(1024, 2048)])
_INV = np.empty(D, np.int64)
_INV[_ORDER] = np.arange(D)

_NC_CACHE = [None]


def kernel(L0_attn_X, L0_attn_W, L0_attn_b, L0_ffn_Wg, L0_ffn_Wu, L0_ffn_Wd,
           L0_ffn_Ga, L0_ffn_Gb, L0_ffn_Ua, L0_ffn_Ub, L0_ffn_Da, L0_ffn_Db,
           L0_ffn_Wr, L0_ffn_temperature):
    X = np.asarray(L0_attn_X, np.float32)
    W = np.asarray(L0_attn_W, np.float32)
    b = np.asarray(L0_attn_b, np.float32)
    Wg = np.asarray(L0_ffn_Wg, np.float32)
    Wu = np.asarray(L0_ffn_Wu, np.float32)
    Wd = np.asarray(L0_ffn_Wd, np.float32)
    Ga = np.asarray(L0_ffn_Ga, np.float32)
    Gb = np.asarray(L0_ffn_Gb, np.float32)
    Ua = np.asarray(L0_ffn_Ua, np.float32)
    Ub = np.asarray(L0_ffn_Ub, np.float32)
    Da = np.asarray(L0_ffn_Da, np.float32)
    Db = np.asarray(L0_ffn_Db, np.float32)
    Wr = np.asarray(L0_ffn_Wr, np.float32)
    temp = np.asarray(L0_ffn_temperature, np.float32)

    o = _ORDER
    Wp = W[o, :]
    wkv = np.ascontiguousarray(
        np.concatenate([Wp[:, 0:D][:, o], Wp[:, D:2 * D][:, o]], axis=1))
    bp = np.concatenate([b[0:D][o], b[D:2 * D][o]])
    bt = np.ascontiguousarray(bp.reshape(32, 128).T)
    wkvl = np.ascontiguousarray(
        wkv.reshape(NCH, 128, 32, 128).transpose(2, 1, 0, 3)
        .reshape(32, 128, NCH * 128))
    g5 = Wg[:, o, :].reshape(NE, NCH, 128, NCH, 128).transpose(0, 3, 2, 1, 4)
    u5 = Wu[:, o, :].reshape(NE, NCH, 128, NCH, 128).transpose(0, 3, 2, 1, 4)
    wgu = np.ascontiguousarray(np.concatenate(
        [g5.reshape(NE, NCH, 128, NCH * 128),
         u5.reshape(NE, NCH, 128, NCH * 128)], axis=3))
    wdl = np.ascontiguousarray(
        Wd[:, :, o].reshape(NE, NCH, 128, 8, 256).transpose(0, 3, 2, 1, 4)
        .reshape(NE, 8, 128, NCH * 256))
    gua = np.ascontiguousarray(np.concatenate(
        [Ga[:, o, :].reshape(NL, NCH, 128, R).transpose(0, 2, 1, 3)
         .reshape(NL, 128, NCH * R),
         Ua[:, o, :].reshape(NL, NCH, 128, R).transpose(0, 2, 1, 3)
         .reshape(NL, 128, NCH * R)], axis=2))
    gub = np.ascontiguousarray(np.concatenate([Gb, Ub], axis=2))
    da = np.ascontiguousarray(
        Da.reshape(NL, NCH, 128, R).transpose(0, 2, 1, 3)
        .reshape(NL, 128, NCH * R))
    db = np.ascontiguousarray(Db[:, :, o])
    wrl = np.ascontiguousarray(
        Wr[o, :].reshape(NCH, 128, 8).transpose(1, 0, 2)
        .reshape(128, NCH * 8))

    pos = np.arange(S, dtype=np.float32)
    inv_freq = (1.0 / (10000.0 ** (np.arange(0, 1024, 2, dtype=np.float32)
                                   / 1024.0))).astype(np.float32)
    freqs = pos[:, None] * inv_freq[None, :]
    ropec = np.cos(freqs).T.astype(np.float32)
    ropes = np.sin(freqs).T.astype(np.float32)
    rope4 = np.ascontiguousarray(np.stack(
        [np.concatenate([ropec[c * 128:(c + 1) * 128],
                         ropes[c * 128:(c + 1) * 128]], axis=1)
         for c in range(4)]))

    if _NC_CACHE[0] is None:
        _NC_CACHE[0] = build_nc()
    nc = _NC_CACHE[0]

    shared = {"wkv": wkvl, "bt": bt, "rope": rope4,
              "wgu": wgu, "wd": wdl, "gua": gua, "gub": gub,
              "da": da, "db": db, "wr": wrl,
              "temp": temp.reshape(1, 1)}
    in_maps = []
    for c in range(8):
        m = dict(shared)
        m["xf"] = np.ascontiguousarray(
            X[c].T[o, :].reshape(NCH, 128, NSUB, 512)
            .transpose(2, 1, 0, 3).reshape(NSUB, 128, NCH * 512))
        in_maps.append(m)

    res = run_bass_kernel_spmd(nc, in_maps, core_ids=list(range(8)))
    out = np.empty((8, S, D), np.float32)
    for c in range(8):
        out[c] = res.results[c]["out"][_INV, :].T
    return out
